# version 05
# baseline (speedup 1.0000x reference)
import numpy as np
import concourse.bass as bass
import concourse.mybir as mybir

F32 = mybir.dt.float32
BF16 = mybir.dt.bfloat16
AF = mybir.ActivationFunctionType
ALU = mybir.AluOpType
AX = mybir.AxisListType

COMPUTE = ("pe", "dve", "act")
QUEUES = ("sp", "pl")
NLANES = 4


class Prog:
    def __init__(self, nc, ctx):
        self.nc = nc
        self.ctx = ctx
        names = list(COMPUTE) + [f"{q}{i}" for q in QUEUES for i in range(NLANES)]
        self.sem = {e: ctx.enter_context(nc.semaphore(f"ch_{e}")) for e in names}
        self.cnt = {e: 0 for e in names}
        self.lane_rr = {q: 0 for q in QUEUES}
        self.last_write = {}
        self.readers = {}
        self.hwm = {c: {p: 0 for p in names} for c in list(COMPUTE) + list(QUEUES)}
        self.names = names
        self.bg = ctx.enter_context(nc.semaphore("ch_bg"))
        self.bg_cnt = 0
        self.bg_hwm = {}
        self.bg_marks = {}

    def bg_dma(self, out_ap, in_ap):
        inst = self.nc.gpsimd.dma_start(out_ap, in_ap)
        self.bg_cnt += 16
        inst.then_inc(self.bg, 16)
        return self.bg_cnt

    def bg_wait(self, consumer, val):
        if val <= 0 or self.bg_hwm.get(consumer, 0) >= val:
            return
        self._engine_obj(consumer).wait_ge(self.bg, val)
        self.bg_hwm[consumer] = val

    def _engine_obj(self, e):
        nc = self.nc
        if e.startswith("sp"):
            return nc.sync
        if e.startswith("pl"):
            return nc.gpsimd
        return {"pe": nc.tensor, "dve": nc.vector, "act": nc.scalar}[e]

    def _need(self, consumer, producer, val):
        if val <= 0:
            return
        if self.hwm[consumer][producer] >= val:
            return
        self._engine_obj(consumer).wait_ge(self.sem[producer], val)
        self.hwm[consumer][producer] = val

    def pre(self, eng, reads=(), writes=()):
        for k in reads:
            lw = self.last_write.get(k)
            if lw is not None:
                self._need(eng, lw[0], lw[1])
        for k in writes:
            lw = self.last_write.get(k)
            if lw is not None:
                self._need(eng, lw[0], lw[1])
            for pe_, v in self.readers.get(k, {}).items():
                self._need(eng, pe_, v)

    def post(self, producer, consumer_key, inst, reads=(), writes=(), inc=1):
        self.cnt[producer] += inc
        c = self.cnt[producer]
        inst.then_inc(self.sem[producer], inc)
        for k in writes:
            self.last_write[k] = (producer, c)
            self.readers[k] = {}
        for k in reads:
            self.readers.setdefault(k, {})[producer] = c

    def op(self, eng, method, *args, reads=(), writes=(), inc=1, **kwargs):
        assert eng in COMPUTE
        self.pre(eng, reads, writes)
        inst = getattr(self._engine_obj(eng), method)(*args, **kwargs)
        self.post(eng, eng, inst, reads, writes, inc=inc)
        return inst

    def dma(self, queue, out_ap, in_ap, reads=(), writes=(), **kw):
        assert queue in QUEUES
        lane = f"{queue}{self.lane_rr[queue]}"
        self.lane_rr[queue] = (self.lane_rr[queue] + 1) % NLANES
        self._need(queue, lane, self.cnt[lane])
        self.pre(queue, reads, writes)
        inst = self._engine_obj(queue).dma_start(out_ap, in_ap, **kw)
        self.post(lane, queue, inst, reads, writes, inc=16)
        return inst

    def mm_group(self, pairs, writes, reads, accumulate=True, flags=None):
        self.pre("pe", reads, writes)
        n = len(pairs)
        inst = None
        for i, (out_ap, lhsT, rhs) in enumerate(pairs):
            if flags is not None:
                st, sp = flags[i]
            elif accumulate:
                st, sp = (i == 0), (i == n - 1)
            else:
                st, sp = True, True
            inst = self.nc.tensor.matmul(out_ap, lhsT, rhs, start=st, stop=sp)
        self.post("pe", "pe", inst, reads, writes, inc=1)
        return inst

    def barrier_all(self):
        for ce in list(COMPUTE) + list(QUEUES):
            for pe_ in self.names:
                if ce != pe_:
                    self._need(ce, pe_, self.cnt[pe_])




D = 1024
DC = 8
TOK = 4096
L = 2048
TCH = 512
NCH = TOK // TCH
INNER = 2048
CT = 16
HF = 2816
HT = 22
NK = 24
NHEAD = 16
DK = 64
EPS = 1e-5


class MK:
    def __init__(self, nc, P: Prog, io: dict):
        self.nc = nc
        self.P = P
        self.io = io

    def ap(self, name):
        return self.io[name].ap()

    def sbuf(self, ctx, name, shape, dtype):
        return ctx.enter_context(self.nc.sbuf_tensor(name, shape, dtype))

    def psum(self, ctx, name, shape, dtype=F32):
        return ctx.enter_context(self.nc.psum_tensor(name, shape, dtype))

    def _ln_chunk(self, b, src_f, kf, src_b, kb, dst, kdst, gam=None, bet=None,
                  kgam="gam", kbet="bet", dst2=None, kdst2=None):
        P = self.P
        ones, sq, psA, psB = b["ones"], b["sq"], b["psA"], b["psB"]
        mean, varr, msq, rstd, tmp = b["mean"], b["var"], b["msq"], b["rstd"], b["tmp"]
        P.op("act", "activation", sq[:], src_b[:], AF.Square, reads=[kb], writes=["sq"])
        P.mm_group([(psA[:], ones[:], src_b[:, dc, :]) for dc in range(DC)],
                   writes=["psA"], reads=[kb, "ones"])
        P.mm_group([(psB[:], ones[:], sq[:, dc, :]) for dc in range(DC)],
                   writes=["psB"], reads=["sq", "ones"])
        P.op("dve", "tensor_scalar_mul", mean[:], psA[:], 1.0 / D,
             reads=["psA"], writes=["mean"])
        P.op("dve", "tensor_scalar_mul", varr[:], psB[:], 1.0 / D,
             reads=["psB"], writes=["var"])
        P.op("dve", "tensor_mul", msq[:], mean[:], mean[:], reads=["mean"], writes=["msq"])
        P.op("dve", "tensor_sub", varr[:], varr[:], msq[:],
             reads=["var", "msq"], writes=["var"])
        P.op("act", "activation", rstd[:], varr[:], AF.Ln, bias=b["epsb"][:, 0:1],
             reads=["var", "epsb"], writes=["rstd"])
        P.op("act", "activation", rstd[:], rstd[:], AF.Exp, scale=-0.5,
             reads=["rstd"], writes=["rstd"])
        for dc in range(DC):
            P.op("dve", "tensor_sub", tmp[:, dc, :], src_f[:, dc, :], mean[:],
                 reads=[kf, "mean"], writes=["tmp"])
            if gam is None:
                if dst2 is not None:
                    P.op("dve", "tensor_mul", dst2[:, dc, :], tmp[:, dc, :], rstd[:],
                         reads=["tmp", "rstd"], writes=[kdst2])
                    P.op("dve", "tensor_copy", dst[:, dc, :], dst2[:, dc, :],
                         reads=[kdst2], writes=[kdst])
                else:
                    P.op("dve", "tensor_mul", dst[:, dc, :], tmp[:, dc, :], rstd[:],
                         reads=["tmp", "rstd"], writes=[kdst])
            else:
                P.op("dve", "tensor_mul", tmp[:, dc, :], tmp[:, dc, :], rstd[:],
                     reads=["tmp", "rstd"], writes=["tmp"])
                P.op("dve", "tensor_scalar", dst[:, dc, :], tmp[:, dc, :],
                     gam[:, dc:dc + 1], bet[:, dc:dc + 1], ALU.mult, ALU.add,
                     reads=["tmp", kgam, kbet], writes=[kdst])

    def _ln_bufs(self, ctx, pref):
        b = {}
        b["ones"] = self.sbuf(ctx, pref + "ones", [128, 128], BF16)
        b["sq"] = self.sbuf(ctx, pref + "sq", [128, DC, TCH], BF16)
        b["mean"] = self.sbuf(ctx, pref + "mean", [128, TCH], F32)
        b["var"] = self.sbuf(ctx, pref + "var", [128, TCH], F32)
        b["msq"] = self.sbuf(ctx, pref + "msq", [128, TCH], F32)
        b["rstd"] = self.sbuf(ctx, pref + "rstd", [128, TCH], F32)
        b["tmp"] = self.sbuf(ctx, pref + "tmp", [128, DC, TCH], F32)
        b["epsb"] = self.sbuf(ctx, pref + "epsb", [128, 1], F32)
        b["psA"] = self.psum(ctx, pref + "psA", [128, TCH])
        b["psB"] = self.psum(ctx, pref + "psB", [128, TCH])
        self.P.op("dve", "memset", b["ones"][:], 1.0, writes=["ones"])
        self.P.op("dve", "memset", b["epsb"][:], EPS, writes=["epsb"])
        return b

    def phase_ln1(self):
        nc, P = self.nc, self.P
        with ExitStack() as ctx:
            b = self._ln_bufs(ctx, "p1_")
            gam = self.sbuf(ctx, "p1_gam", [128, DC], F32)
            bet = self.sbuf(ctx, "p1_bet", [128, DC], F32)
            x0c = self.sbuf(ctx, "p1_x0c", [128, DC, TCH], F32)
            x0b = self.sbuf(ctx, "p1_x0b", [128, DC, TCH], BF16)
            xn1c = self.sbuf(ctx, "p1_xn1c", [128, DC, TCH], BF16)
            xn1f = self.sbuf(ctx, "p1_xn1f", [128, DC, TCH], F32)
            hc = self.sbuf(ctx, "p1_hc", [128, DC, TCH], BF16)
            P.dma("sp", gam[:], self.ap("ln0_gamma_t"), writes=["gam"])
            P.dma("sp", bet[:], self.ap("ln0_beta_t"), writes=["bet"])
            P.bg_marks = getattr(P, "bg_marks", {})
            P.bg_marks["in_w_bf"] = P.bg_dma(
                self.ap("in_w_bf"),
                self.ap("in_w").rearrange("(c p) f -> p c f", p=128))
            for e in range(8):
                P.bg_dma(self.ap("w1b")[e],
                         self.ap("w1T")[e].rearrange("(c p) h -> p c h", p=128))
                P.bg_dma(self.ap("w3b")[e],
                         self.ap("w3T")[e].rearrange("(c p) h -> p c h", p=128))
                P.bg_marks[("w", e)] = P.bg_dma(
                    self.ap("w2b")[e],
                    self.ap("w2T")[e].rearrange("(t p) f -> p t f", p=128))
            x0T = self.ap("x0T").rearrange("(c p) t -> p c t", p=128)
            xn1T = self.ap("xn1T").rearrange("(c p) t -> p c t", p=128)
            xn1fT = self.ap("xn1fT").rearrange("(c p) t -> p c t", p=128)
            hT = self.ap("hT").rearrange("(c p) t -> p c t", p=128)
            for ch in range(NCH):
                t0 = ch * TCH
                P.dma("sp", x0c[:], x0T[:, :, t0:t0 + TCH], writes=["x0c"])
                P.op("act", "activation", x0b[:], x0c[:], AF.Copy,
                     reads=["x0c"], writes=["x0b"])
                self._ln_chunk(b, x0c, "x0c", x0b, "x0b", xn1c, "xn1c",
                               dst2=xn1f, kdst2="xn1f")
                P.dma("sp", xn1T[:, :, t0:t0 + TCH], xn1c[:], reads=["xn1c"])
                P.dma("sp", xn1fT[:, :, t0:t0 + TCH], xn1f[:], reads=["xn1f"])
                self._ln_chunk(b, xn1c, "xn1c", xn1c, "xn1c", hc, "hc",
                               gam=gam, bet=bet)
                P.dma("sp", hT[:, :, t0:t0 + TCH], hc[:], reads=["hc"])
            P.barrier_all()

    def phase_mamba(self):
        nc, P = self.nc, self.P
        hT = self.ap("hT").rearrange("(c p) t -> p c t", p=128)
        xn1T = self.ap("xn1T").rearrange("(c p) t -> p c t", p=128)
        sbrT = self.ap("sbrT").rearrange("(c p) t -> p c t", p=128)
        with ExitStack() as ctx:
            us = self.sbuf(ctx, "p2_us", [128, CT, 32 + L], BF16)

            K_sb = self.sbuf(ctx, "p2_K", [128, CT, NK], F32)
            Kb = self.sbuf(ctx, "p2_Kb", [128, CT, NK], BF16)
            idnk = self.sbuf(ctx, "p2_idnk", [128, NK, 128], BF16)
            cw = self.sbuf(ctx, "p2_cw", [128, CT, 3], BF16)
            cb = self.sbuf(ctx, "p2_cb", [128, CT], F32)
            inb = self.sbuf(ctx, "p2_inb", [128, 48], F32)
            tgb = self.sbuf(ctx, "p2_tgb", [128, CT], F32)
            Dp = self.sbuf(ctx, "p2_Dp", [128, CT], F32)
            outb = self.sbuf(ctx, "p2_outb", [128, DC], F32)

            P.dma("sp", idnk[:], self.ap("identnk"), writes=["idnk"])
            kctx = ExitStack()
            eA = self.sbuf(kctx, "p2_eA", [128, 1024], F32)
            R = self.sbuf(kctx, "p2_R", [128, 1024], F32)
            Cp = self.sbuf(kctx, "p2_C", [128, 1024], F32)
            P.dma("sp", eA[:], self.ap("A_log_t"), writes=["eA"])
            P.dma("sp", R[:], self.ap("B_p_t"), writes=["R"])
            P.dma("sp", Cp[:], self.ap("C_p_t"), writes=["Cp"])
            P.dma("pl", cw[:], self.ap("conv_w_t"), writes=["cw"])
            P.dma("sp", cb[:], self.ap("conv_b_t"), writes=["cb"])
            P.dma("sp", inb[:], self.ap("in_b_t"), writes=["inb"])
            P.dma("sp", Dp[:], self.ap("D_p_t"), writes=["Dp"])
            P.dma("sp", outb[:], self.ap("out_b_t"), writes=["outb"])

            P.op("act", "activation", eA[:], eA[:], AF.Exp, reads=["eA"], writes=["eA"])
            P.op("act", "activation", eA[:], eA[:], AF.Exp, scale=-1.0,
                 reads=["eA"], writes=["eA"])
            P.op("dve", "tensor_mul", R[:], R[:], Cp[:], reads=["R", "Cp"], writes=["R"])
            Rv = R[:].rearrange("p (c n) -> p c n", n=64)
            for d in range(NK):
                if d > 0:
                    P.op("dve", "tensor_mul", R[:], R[:], eA[:], reads=["R", "eA"],
                         writes=["R"])
                P.op("dve", "tensor_reduce", K_sb[:, :, d], Rv, axis=AX.X, op=ALU.add,
                     reads=["R"], writes=["K"])
            P.op("dve", "tensor_add", K_sb[:, :, 0], K_sb[:, :, 0], Dp[:],
                 reads=["K", "Dp"], writes=["K"])
            P.op("dve", "tensor_copy", Kb[:], K_sb[:], reads=["K"], writes=["Kb"])
            P.op("dve", "tensor_scalar_mul", tgb[:], inb[:, 32:48], 0.5,
                 reads=["inb"], writes=["tgb"])
            P.barrier_all()
            kctx.close()

            for bi in range(2):
                tb = bi * L
                with ExitStack() as ctxA:
                    u_raw = self.sbuf(ctxA, f"p2_uraw_{bi}", [128, CT, L + 2], BF16)
                    hc = self.sbuf(ctxA, f"p2_hcA_{bi}", [128, DC, TCH], BF16)
                    iw = self.sbuf(ctxA, f"p2_iwA_{bi}", [128, DC, 2048], BF16)
                    d3 = self.sbuf(ctxA, f"p2_d3_{bi}", [128, 2, 3, 128], BF16)
                    psA = self.psum(ctxA, f"p2_psA_{bi}", [128, 2, TCH])
                    psC = self.psum(ctxA, f"p2_psC_{bi}", [128, 2, TCH])
                    P.bg_wait("sp", P.bg_marks.get("in_w_bf", 0))
                    P.dma("sp", iw[:], self.ap("in_w_bf")[:, :, 0:2048],
                          writes=["iwA"])
                    P.op("dve", "memset", u_raw[:, :, 0:1], 0.0, writes=["uraw_pad"])
                    P.op("dve", "memset", u_raw[:, :, L + 1:L + 2], 0.0, writes=["uraw_pad"])
                    P.op("dve", "memset", us[:, :, 0:32], 0.0, writes=["us_pad"])
                    for cc in range(4):
                        t0 = cc * TCH
                        P.dma("sp", hc[:], hT[:, :, tb + t0: tb + t0 + TCH], writes=["hc"])
                        for ft in range(16):
                            bsl = ft % 2
                            P.mm_group([(psA[:, bsl, :],
                                         iw[:, dc, ft * 128:(ft + 1) * 128],
                                         hc[:, dc, :]) for dc in range(DC)],
                                       writes=[f"psA{bsl}"], reads=["iwA", "hc"])
                            P.op("act", "activation",
                                 u_raw[:, ft, 1 + t0:1 + t0 + TCH], psA[:, bsl, :],
                                 AF.Identity, bias=inb[:, ft:ft + 1],
                                 reads=[f"psA{bsl}", "inb"], writes=["uraw"])
                    for cc in range(4):
                        t0 = cc * TCH
                        for ct in range(CT):
                            r = ct % 2
                            P.op("dve", "tensor_tensor", d3[:, r], idnk[:, 0:3, :],
                                 cw[:, ct, :].unsqueeze(2).broadcast_to((128, 3, 128)),
                                 ALU.mult, reads=["idnk", "cw"], writes=[f"d3{r}"])
                            P.mm_group([(psC[:, r, :], d3[:, r, j, :],
                                         u_raw[:, ct, t0 + j:t0 + j + TCH])
                                        for j in range(3)],
                                       writes=[f"psC{r}"], reads=[f"d3{r}", "uraw"])
                            P.op("act", "activation", us[:, ct, 32 + t0:32 + t0 + TCH],
                                 psC[:, r, :], AF.Silu, bias=cb[:, ct:ct + 1],
                                 reads=[f"psC{r}", "cb"], writes=["us"])
                    if getattr(self, "mamba_dbg", None) == "us" and bi == 0:
                        P.dma("sp", self.io["dbgT"].ap(), us[:].rearrange("p c t -> p (c t)"), reads=["us"])
                    P.barrier_all()
                with ExitStack() as ctxB:
                    out_w_sb = self.sbuf(ctxB, f"p2_outw_{bi}", [128, CT, 1024], BF16)
                    hc = self.sbuf(ctxB, f"p2_hcB_{bi}", [128, DC, TCH], BF16)
                    xc = self.sbuf(ctxB, f"p2_xc_{bi}", [128, DC, TCH], BF16)
                    iw = self.sbuf(ctxB, f"p2_iwB_{bi}", [128, 2, DC, 256], BF16)
                    P.dma("pl", out_w_sb[:],
                          self.ap("out_w").rearrange("(ct p) f -> p ct f", p=128),
                          writes=["outw"])
                    sv = self.sbuf(ctxB, f"p2_sv_{bi}", [128, CT, TCH], BF16)
                    th = self.sbuf(ctxB, f"p2_th_{bi}", [128, CT, TCH], BF16)
                    dg = self.sbuf(ctxB, f"p2_dg_{bi}", [128, 2, NK, 128], BF16)
                    ym = self.sbuf(ctxB, f"p2_ym_{bi}", [128, CT, TCH], BF16)
                    t1 = self.sbuf(ctxB, f"p2_t1_{bi}", [128, TCH], F32)
                    sbc = self.sbuf(ctxB, f"p2_sbc_{bi}", [128, DC, TCH], BF16)
                    psB = self.psum(ctxB, f"p2_psB_{bi}", [128, 2, TCH])
                    psY = self.psum(ctxB, f"p2_psY_{bi}", [128, 2, TCH])
                    psM = self.psum(ctxB, f"p2_psM_{bi}", [128, 4, TCH])
                    for cc in range(4):
                        t0 = cc * TCH
                        P.dma("sp", hc[:], hT[:, :, tb + t0: tb + t0 + TCH], writes=["hc"])
                        P.dma("sp", xc[:], xn1T[:, :, tb + t0: tb + t0 + TCH], writes=["xc"])
                        for fb in range(16):
                            sl = fb % 2
                            P.dma("sp", iw[:, sl],
                                  self.ap("in_w_bf")[:, :, 2048 + fb * 256:
                                                     2048 + (fb + 1) * 256],
                                  writes=[f"iw{sl}"])
                            for j in range(2):
                                ft = 16 + fb * 2 + j
                                bsl = ft % 2
                                P.mm_group([(psB[:, bsl, :],
                                             iw[:, sl, dc, j * 128:(j + 1) * 128],
                                             hc[:, dc, :]) for dc in range(DC)],
                                           writes=[f"psB{bsl}"], reads=[f"iw{sl}", "hc"])
                                if ft < 32:
                                    P.op("act", "activation", sv[:, ft - 16, :],
                                         psB[:, bsl, :], AF.Silu, bias=inb[:, ft:ft + 1],
                                         reads=[f"psB{bsl}", "inb"], writes=["sv"])
                                else:
                                    P.op("act", "activation", th[:, ft - 32, :],
                                         psB[:, bsl, :], AF.Tanh, scale=0.5,
                                         bias=tgb[:, ft - 32:ft - 31],
                                         reads=[f"psB{bsl}", "tgb"], writes=["th"])
                        for ct in range(CT):
                            ysl = ct % 2
                            P.op("dve", "tensor_tensor", dg[:, ysl], idnk[:],
                                 Kb[:, ct, :].unsqueeze(2).broadcast_to((128, NK, 128)),
                                 ALU.mult, reads=["idnk", "Kb"], writes=[f"dg{ysl}"])
                            P.mm_group([(psY[:, ysl, :], dg[:, ysl, d, :],
                                         us[:, ct, 32 + t0 - d:32 + t0 - d + TCH])
                                        for d in range(NK)],
                                       writes=[f"psY{ysl}"], reads=[f"dg{ysl}", "us"])
                            P.op("dve", "tensor_mul", t1[:], psY[:, ysl, :], th[:, ct, :],
                                 reads=[f"psY{ysl}", "th"], writes=["t1"])
                            P.op("dve", "tensor_add", t1[:], t1[:], psY[:, ysl, :],
                                 reads=["t1", f"psY{ysl}"], writes=["t1"])
                            P.op("dve", "scalar_tensor_tensor", ym[:, ct, :], t1[:], 0.5,
                                 sv[:, ct, :], ALU.mult, ALU.add,
                                 reads=["t1", "sv"], writes=["ym"])
                        for fh in range(2):
                            for f in range(4):
                                fo = fh * 4 + f
                                P.mm_group([(psM[:, f, :],
                                             out_w_sb[:, ct2, fo * 128:(fo + 1) * 128],
                                             ym[:, ct2, :]) for ct2 in range(CT)],
                                           writes=[f"psM{f}"], reads=["outw", "ym"])
                            for f in range(4):
                                fo = fh * 4 + f
                                P.op("dve", "scalar_tensor_tensor", sbc[:, fo, :],
                                     psM[:, f, :], outb[:, fo:fo + 1], xc[:, fo, :],
                                     ALU.add, ALU.add,
                                     reads=[f"psM{f}", "outb", "xc"], writes=["sbc"])
                        P.dma("sp", sbrT[:, :, tb + t0: tb + t0 + TCH], sbc[:],
                              reads=["sbc"])
                    P.barrier_all()
            P.barrier_all()

    def _gate_phase(self, which):
        nc, P = self.nc, self.P
        srcT = self.ap("xn1fT" if which == 0 else "xn2fT").rearrange("(c p) t -> p c t", p=128)
        dstT = self.ap("comb0T" if which == 0 else "comb1T")
        gwname = "gate_w_T" if which == 0 else "gate_W"
        pr = f"g{which}_"
        with ExitStack() as ctx:
            gw = self.sbuf(ctx, pr + "gw", [128, DC, 8], F32)
            xc = self.sbuf(ctx, pr + "xc", [128, DC, TCH], F32)
            ident = self.sbuf(ctx, pr + "id", [128, 128], F32)
            cT = self.sbuf(ctx, pr + "cT", [8, TOK], F32)
            sc = self.sbuf(ctx, pr + "sc", [128, 8], F32)
            mx = self.sbuf(ctx, pr + "mx", [128, 1], F32)
            p = self.sbuf(ctx, pr + "p", [128, 8], F32)
            m1 = self.sbuf(ctx, pr + "m1", [128, 8], F32)
            m2 = self.sbuf(ctx, pr + "m2", [128, 8], F32)
            sm = self.sbuf(ctx, pr + "sm", [128, 1], F32)
            psG = self.psum(ctx, pr + "psG", [128, 8])
            psT = self.psum(ctx, pr + "psT", [8, 128])
            P.dma("sp", gw[:], self.ap(gwname).rearrange("(c p) e -> p c e", p=128),
                  writes=["gw"])
            P.dma("sp", ident[:], self.ap("ident"), writes=["ident"])
            on1 = self.sbuf(ctx, pr + "on1", [1, 128], F32)
            tbr = self.sbuf(ctx, pr + "tbr", [1, 8], F32)
            tbb = self.sbuf(ctx, pr + "tbb", [128, 8], F32)
            ptb = self.sbuf(ctx, pr + "ptb", [128, 8], F32)
            psBB = self.psum(ctx, pr + "psBB", [128, 8])
            P.dma("sp", on1[:], self.ap("ones1_f32"), writes=["on1"])
            P.dma("sp", tbr[:], self.ap("tiebk"), writes=["tbr"])
            P.mm_group([(psBB[:], on1[:], tbr[:])], writes=["psBB"],
                       reads=["on1", "tbr"])
            P.op("act", "activation", tbb[:], psBB[:], AF.Identity,
                 reads=["psBB"], writes=["tbb"])
            if which == 1:
                gbr = self.sbuf(ctx, pr + "gbr", [1, 8], F32)
                gbb = self.sbuf(ctx, pr + "gbb", [128, 8], F32)
                P.dma("sp", gbr[:], self.ap("gate_b_row"), writes=["gbr"])
                P.mm_group([(psBB[:], on1[:], gbr[:])], writes=["psBB"],
                           reads=["on1", "gbr"])
                P.op("act", "activation", gbb[:], psBB[:], AF.Identity,
                     reads=["psBB"], writes=["gbb"])
            for ch in range(NCH):
                t0 = ch * TCH
                P.dma("sp", xc[:], srcT[:, :, t0:t0 + TCH], writes=["xc"])
                for tt in range(TCH // 128):
                    tok0 = t0 + tt * 128
                    P.mm_group([(psG[:], xc[:, dc, tt * 128:(tt + 1) * 128], gw[:, dc, :])
                                for dc in range(DC)], writes=["psG"], reads=["xc", "gw"])
                    if which == 0:
                        P.op("act", "activation", sc[:], psG[:], AF.Identity,
                             reads=["psG"], writes=["sc"])
                    else:
                        P.op("dve", "tensor_add", sc[:], psG[:], gbb[:],
                             reads=["psG", "gbb"], writes=["sc"])
                        P.op("dve", "tensor_scalar_max", sc[:], sc[:], 1e-4,
                             reads=["sc"], writes=["sc"])
                    P.op("dve", "reduce_max", mx[:, 0:1], sc[:], axis=AX.X,
                         reads=["sc"], writes=["mx"])
                    P.op("dve", "tensor_scalar", p[:], sc[:], mx[:, 0:1], None,
                         ALU.subtract, reads=["sc", "mx"], writes=["p"])
                    P.op("act", "activation", p[:], p[:], AF.Exp,
                         reads=["p"], writes=["p"])
                    P.op("dve", "reduce_sum", sm[:, 0:1], p[:], axis=AX.X,
                         reads=["p"], writes=["sm"])
                    P.op("dve", "reciprocal", sm[:, 0:1], sm[:, 0:1],
                         reads=["sm"], writes=["sm"])
                    P.op("dve", "tensor_scalar_mul", p[:], p[:], sm[:, 0:1],
                         reads=["p", "sm"], writes=["p"])
                    P.op("dve", "tensor_add", ptb[:], p[:], tbb[:],
                         reads=["p", "tbb"], writes=["ptb"])
                    P.op("dve", "reduce_max", mx[:, 0:1], ptb[:], axis=AX.X,
                         reads=["ptb"], writes=["mx"])
                    P.op("dve", "tensor_scalar", m1[:], ptb[:], mx[:, 0:1], None,
                         ALU.is_ge, reads=["ptb", "mx"], writes=["m1"])
                    P.op("dve", "scalar_tensor_tensor", m2[:], m1[:], -1e30, ptb[:],
                         ALU.mult, ALU.add, reads=["m1", "ptb"], writes=["m2"])
                    P.op("dve", "reduce_max", mx[:, 0:1], m2[:], axis=AX.X,
                         reads=["m2"], writes=["mx"])
                    P.op("dve", "tensor_scalar", m2[:], m2[:], mx[:, 0:1], None,
                         ALU.is_ge, reads=["m2", "mx"], writes=["m2"])
                    P.op("dve", "tensor_max", m1[:], m1[:], m2[:],
                         reads=["m1", "m2"], writes=["m1"])
                    if which == 1:
                        P.op("dve", "tensor_scalar_max", p[:], p[:], 0.1,
                             reads=["p"], writes=["p"])
                    P.op("dve", "tensor_mul", p[:], p[:], m1[:],
                         reads=["p", "m1"], writes=["p"])
                    P.op("dve", "reduce_sum", sm[:, 0:1], p[:], axis=AX.X,
                         reads=["p"], writes=["sm"])
                    P.op("dve", "reciprocal", sm[:, 0:1], sm[:, 0:1],
                         reads=["sm"], writes=["sm"])
                    if which == 1:
                        P.op("dve", "tensor_scalar_mul", sm[:, 0:1], sm[:, 0:1], 0.5,
                             reads=["sm"], writes=["sm"])
                    P.op("dve", "tensor_scalar_mul", p[:], p[:], sm[:, 0:1],
                         reads=["p", "sm"], writes=["p"])
                    dbg = getattr(self, "gate_dbg", None)
                    if dbg == "sc":
                        P.op("dve", "tensor_copy", p[:], sc[:], reads=["sc"], writes=["p"])
                    elif dbg == "m1":
                        P.op("dve", "tensor_copy", p[:], m1[:], reads=["m1"], writes=["p"])
                    P.op("pe", "transpose", psT[:, :], p[:], ident[:],
                         reads=["p", "ident"], writes=["psT"])
                    P.op("dve", "tensor_copy", cT[:, tok0:tok0 + 128], psT[0:8, :],
                         reads=["psT"], writes=["cT"])
            P.dma("sp", dstT, cT[:], reads=["cT"])
            P.barrier_all()

    def phase_gate0(self):
        self._gate_phase(0)

    def phase_gate1(self):
        self._gate_phase(1)

    def phase_moe0(self):
        nc, P = self.nc, self.P
        xn1T = self.ap("xn1T").rearrange("(c p) t -> p c t", p=128)
        accT = self.ap("accT").rearrange("(c p) t -> p c t", p=128)
        with ExitStack() as ctx:
            sel = self.sbuf(ctx, "p4_sel", [8, 1024], F32)
            cT = self.sbuf(ctx, "p4_cT", [8, TOK], F32)
            P.dma("sp", sel[:], self.ap("sel"), writes=["sel"])
            P.dma("sp", cT[:], self.ap("comb0T"), writes=["cT"])
            NE = getattr(self, "moe0_experts", 8)
            NCHX = getattr(self, "moe0_chunks", NCH)
            for e in range(NE):
                with ExitStack() as ctxE:
                    w1s = self.sbuf(ctxE, f"p4_w1_{e}", [128, DC, HF], BF16)
                    w3s = self.sbuf(ctxE, f"p4_w3_{e}", [128, DC, HF], BF16)
                    w2s = self.sbuf(ctxE, f"p4_w2_{e}", [128, HT, 1024], BF16)
                    xc = self.sbuf(ctxE, f"p4_xc_{e}", [128, DC, TCH], BF16)
                    cbc = self.sbuf(ctxE, f"p4_cbc_{e}", [128, TCH], F32)
                    h1s = self.sbuf(ctxE, f"p4_h1s_{e}", [128, 2, TCH], BF16)
                    h3s = self.sbuf(ctxE, f"p4_h3s_{e}", [128, 2, TCH], BF16)
                    H = self.sbuf(ctxE, f"p4_H_{e}", [128, HT, TCH], BF16)
                    accc = self.sbuf(ctxE, f"p4_acc_{e}", [128, DC, TCH], F32)
                    ps8 = self.psum(ctxE, f"p4_ps_{e}", [128, 8, TCH])
                    P.bg_wait("sp", P.bg_marks.get(("w", e), 0))
                    P.dma("sp", w1s[:], self.ap("w1b")[e], writes=["w1s"])
                    P.dma("sp", w3s[:], self.ap("w3b")[e], writes=["w3s"])
                    P.dma("sp", w2s[:], self.ap("w2b")[e], writes=["w2s"])
                    for ch in range(NCHX):
                        t0 = ch * TCH
                        P.dma("sp", xc[:], xn1T[:, :, t0:t0 + TCH], writes=["xc"])
                        if e > 0:
                            P.dma("sp", accc[:], accT[:, :, t0:t0 + TCH], writes=["accc"])
                        P.mm_group([(ps8[:, 4, :], sel[0:8, e * 128:(e + 1) * 128],
                                     cT[0:8, t0:t0 + TCH])],
                                   writes=["b4"], reads=["sel", "cT"])
                        P.op("act", "activation", cbc[:], ps8[:, 4, :], AF.Identity,
                             reads=["b4"], writes=["cbc"])
                        for ht in range(HT):
                            r = ht % 2
                            ba, bb = 2 * r, 2 * r + 1
                            P.mm_group([(ps8[:, ba, :], w1s[:, dc, ht * 128:(ht + 1) * 128],
                                         xc[:, dc, :]) for dc in range(DC)],
                                       writes=[f"b{ba}"], reads=["w1s", "xc"])
                            P.mm_group([(ps8[:, bb, :], w3s[:, dc, ht * 128:(ht + 1) * 128],
                                         xc[:, dc, :]) for dc in range(DC)],
                                       writes=[f"b{bb}"], reads=["w3s", "xc"])
                            P.op("act", "activation", h1s[:, r, :], ps8[:, ba, :], AF.Silu,
                                 reads=[f"b{ba}"], writes=[f"h1s{r}"])
                            P.op("dve", "tensor_mul", h3s[:, r, :], ps8[:, bb, :], cbc[:],
                                 reads=[f"b{bb}", "cbc"], writes=[f"h3s{r}"])
                            P.op("dve", "tensor_mul", H[:, ht, :], h1s[:, r, :], h3s[:, r, :],
                                 reads=[f"h1s{r}", f"h3s{r}"], writes=["H"])
                        for f in range(8):
                            P.mm_group([(ps8[:, f, :], w2s[:, ht2, f * 128:(f + 1) * 128],
                                         H[:, ht2, :]) for ht2 in range(HT)],
                                       writes=[f"b{f}"], reads=["w2s", "H"])
                        for f in range(8):
                            if e == 0:
                                P.op("act", "activation", accc[:, f, :], ps8[:, f, :],
                                     AF.Identity, reads=[f"b{f}"], writes=["accc"])
                            else:
                                P.op("dve", "tensor_add", accc[:, f, :], ps8[:, f, :],
                                     accc[:, f, :], reads=[f"b{f}", "accc"],
                                     writes=["accc"])
                        P.dma("sp", accT[:, :, t0:t0 + TCH], accc[:], reads=["accc"])
                    P.barrier_all()
            P.barrier_all()

    def phase_combine0(self):
        nc, P = self.nc, self.P
        x0T = self.ap("x0T").rearrange("(c p) t -> p c t", p=128)
        sbrT = self.ap("sbrT").rearrange("(c p) t -> p c t", p=128)
        accT = self.ap("accT").rearrange("(c p) t -> p c t", p=128)
        x1T = self.ap("x1T").rearrange("(c p) t -> p c t", p=128)
        xn2T = self.ap("xn2T").rearrange("(c p) t -> p c t", p=128)
        xn2fT = self.ap("xn2fT").rearrange("(c p) t -> p c t", p=128)
        with ExitStack() as ctx:
            x0c = self.sbuf(ctx, "p5_x0c", [128, DC, TCH], F32)
            sbc = self.sbuf(ctx, "p5_sbc", [128, DC, TCH], BF16)
            acc = self.sbuf(ctx, "p5_acc", [128, DC, TCH], F32)
            g = self.sbuf(ctx, "p5_g", [128, DC, TCH], F32)
            x1c = self.sbuf(ctx, "p5_x1c", [128, DC, TCH], F32)
            for ch in range(NCH):
                t0 = ch * TCH
                P.dma("sp", x0c[:], x0T[:, :, t0:t0 + TCH], writes=["x0c"])
                P.dma("sp", sbc[:], sbrT[:, :, t0:t0 + TCH], writes=["sbc"])
                P.dma("sp", acc[:], accT[:, :, t0:t0 + TCH], writes=["acc"])
                P.op("act", "activation", g[:], acc[:], AF.Silu,
                     reads=["acc"], writes=["g"])
                P.op("dve", "tensor_mul", g[:], g[:], sbc[:],
                     reads=["g", "sbc"], writes=["g"])
                P.op("dve", "tensor_add", x1c[:], x0c[:], g[:],
                     reads=["x0c", "g"], writes=["x1c"])
                P.dma("sp", x1T[:, :, t0:t0 + TCH], x1c[:], reads=["x1c"])
            P.barrier_all()
        with ExitStack() as ctx:
            b = self._ln_bufs(ctx, "p5b_")
            x1c = self.sbuf(ctx, "p5b_x1c", [128, DC, TCH], F32)
            x1b = self.sbuf(ctx, "p5b_x1b", [128, DC, TCH], BF16)
            xn2c = self.sbuf(ctx, "p5b_xn2c", [128, DC, TCH], BF16)
            xn2f = self.sbuf(ctx, "p5b_xn2f", [128, DC, TCH], F32)
            for ch in range(NCH):
                t0 = ch * TCH
                P.dma("sp", x1c[:], x1T[:, :, t0:t0 + TCH], writes=["x1c2"])
                P.op("act", "activation", x1b[:], x1c[:], AF.Copy,
                     reads=["x1c2"], writes=["x1b"])
                self._ln_chunk(b, x1c, "x1c2", x1b, "x1b", xn2c, "xn2c",
                               dst2=xn2f, kdst2="xn2f")
                P.dma("sp", xn2T[:, :, t0:t0 + TCH], xn2c[:], reads=["xn2c"])
                P.dma("sp", xn2fT[:, :, t0:t0 + TCH], xn2f[:], reads=["xn2f"])
            P.barrier_all()

    def phase_moe1(self):
        nc, P = self.nc, self.P
        xn2T = self.ap("xn2T").rearrange("(c p) t -> p c t", p=128)
        g2T = self.ap("g2T").rearrange("(c p) t -> p c t", p=128)
        with ExitStack() as ctx:
            ew = self.sbuf(ctx, "p8_ew", [128, 8, DC, 1024], BF16)
            eb = self.sbuf(ctx, "p8_eb", [8, 1024], BF16)
            sel = self.sbuf(ctx, "p8_sel", [8, 1024], F32)
            cT = self.sbuf(ctx, "p8_cT", [8, TOK], F32)
            cTb = self.sbuf(ctx, "p8_cTb", [8, TOK], BF16)
            xc = self.sbuf(ctx, "p8_xc", [128, DC, TCH], BF16)
            cbc = self.sbuf(ctx, "p8_cbc", [128, 8, TCH], F32)
            xe = self.sbuf(ctx, "p8_xe", [128, 2, DC, TCH], BF16)
            g2c = self.sbuf(ctx, "p8_g2c", [128, DC, TCH], BF16)
            ps8 = self.psum(ctx, "p8_ps", [128, 8, TCH])
            P.dma("pl", ew[:], self.ap("expert_W").rearrange("e (c p) f -> p e c f", p=128),
                  writes=["ew"])
            P.dma("pl", eb[:], self.ap("expert_b"), writes=["eb"])
            P.dma("sp", sel[:], self.ap("sel"), writes=["sel"])
            P.dma("sp", cT[:], self.ap("comb1T"), writes=["cT"])
            P.dma("pl", cTb[:], self.ap("comb1T"), writes=["cTb"])
            for ch in range(NCH):
                t0 = ch * TCH
                P.dma("sp", xc[:], xn2T[:, :, t0:t0 + TCH], writes=["xc"])
                for e in range(8):
                    P.mm_group([(ps8[:, 7, :], sel[0:8, e * 128:(e + 1) * 128],
                                 cT[0:8, t0:t0 + TCH])],
                               writes=["b7"], reads=["sel", "cT"])
                    P.op("act", "activation", cbc[:, e, :], ps8[:, 7, :], AF.Identity,
                         reads=["b7"], writes=["cbc"])
                for e in range(8):
                    es = e % 2
                    for dc in range(DC):
                        P.op("dve", "tensor_mul", xe[:, es, dc, :], xc[:, dc, :],
                             cbc[:, e, :], reads=["xc", "cbc"], writes=[f"xe{es}"])
                    for f in range(8):
                        P.mm_group([(ps8[:, f, :], ew[:, e, dc, f * 128:(f + 1) * 128],
                                     xe[:, es, dc, :]) for dc in range(DC)],
                                   writes=[f"b{f}"], reads=["ew", f"xe{es}"],
                                   flags=[((e == 0 and dc2 == 0), False)
                                          for dc2 in range(DC)])
                for f in range(8):
                    P.mm_group([(ps8[:, f, :], eb[0:8, f * 128:(f + 1) * 128],
                                 cTb[0:8, t0:t0 + TCH])],
                               writes=[f"b{f}"], reads=["eb", "cTb"],
                               flags=[(False, True)])
                for f in range(8):
                    P.op("act", "activation", g2c[:, f, :], ps8[:, f, :], AF.Silu,
                         reads=[f"b{f}"], writes=["g2c"])
                P.dma("sp", g2T[:, :, t0:t0 + TCH], g2c[:], reads=["g2c"])
            P.barrier_all()

    def phase_final(self):
        nc, P = self.nc, self.P
        x1T = self.ap("x1T").rearrange("(c p) t -> p c t", p=128)
        attoT = self.ap("attoT").rearrange("(c p) t -> p c t", p=128)
        g2T = self.ap("g2T").rearrange("(c p) t -> p c t", p=128)
        outT = self.ap("outT").rearrange("(c p) t -> p c t", p=128)
        with ExitStack() as ctx:
            x1c = self.sbuf(ctx, "p9_x1c", [128, DC, TCH], F32)
            atc = self.sbuf(ctx, "p9_atc", [128, DC, TCH], BF16)
            g2c = self.sbuf(ctx, "p9_g2c", [128, DC, TCH], BF16)
            t = self.sbuf(ctx, "p9_t", [128, DC, TCH], F32)
            oc = self.sbuf(ctx, "p9_oc", [128, DC, TCH], F32)
            for ch in range(NCH):
                t0 = ch * TCH
                P.dma("sp", x1c[:], x1T[:, :, t0:t0 + TCH], writes=["x1c"])
                P.dma("sp", atc[:], attoT[:, :, t0:t0 + TCH], writes=["atc"])
                P.dma("sp", g2c[:], g2T[:, :, t0:t0 + TCH], writes=["g2c"])
                P.op("dve", "tensor_mul", t[:], atc[:], g2c[:],
                     reads=["atc", "g2c"], writes=["t"])
                P.op("dve", "tensor_add", oc[:], x1c[:], t[:],
                     reads=["x1c", "t"], writes=["oc"])
                P.dma("sp", outT[:, :, t0:t0 + TCH], oc[:], reads=["oc"])
            P.barrier_all()

    def phase_attn(self):
        nc, P = self.nc, self.P
        xn2T = self.ap("xn2T").rearrange("(c p) t -> p c t", p=128)
        attoT = self.ap("attoT").rearrange("(c p) t -> p c t", p=128)
        QC = 4
        with ExitStack() as ctx:
            bq = self.sbuf(ctx, "p7_bq", [128, DC], F32)
            bk = self.sbuf(ctx, "p7_bk", [128, DC], F32)
            bo = self.sbuf(ctx, "p7_bo", [128, DC], F32)
            bvb = self.sbuf(ctx, "p7_bvb", [1, 1024], BF16)
            Erb = self.sbuf(ctx, "p7_Erb", [128, L], BF16)
            on_b = self.sbuf(ctx, "p7_onb", [128, 128], BF16)
            on_f = self.sbuf(ctx, "p7_onf", [128, 64], F32)
            P.dma("sp", bq[:], self.ap("bq_t"), writes=["bq"])
            P.dma("sp", bk[:], self.ap("bk_t"), writes=["bk"])
            P.dma("sp", bo[:], self.ap("bo_t"), writes=["bo"])
            P.dma("pl", bvb[:], self.ap("bv_row"), writes=["bvb"])
            P.dma("pl", Erb[0:64, :], self.ap("ErT"), writes=["Erb"])
            P.dma("pl", Erb[64:128, :], self.ap("ErT"), writes=["Erb"])
            P.op("dve", "memset", on_b[:], 1.0, writes=["onb"])
            P.op("dve", "memset", on_f[:], 1.0, writes=["onf"])
            for bi in range(2):
                tb = bi * L
                with ExitStack() as ctxB:
                    Wq = self.sbuf(ctxB, f"p7_Wq_{bi}", [128, DC, 1024], BF16)
                    Wk = self.sbuf(ctxB, f"p7_Wk_{bi}", [128, DC, 1024], BF16)
                    Wv = self.sbuf(ctxB, f"p7_Wv_{bi}", [128, DC, 1024], BF16)
                    Wo = self.sbuf(ctxB, f"p7_Wo_{bi}", [128, DC, 1024], BF16)
                    for nm, t in [("WqT", Wq), ("WkT", Wk), ("WvT", Wv), ("WoT", Wo)]:
                        P.dma("pl", t[:],
                              self.ap(nm).rearrange("(c p) f -> p c f", p=128),
                              writes=[nm])
                    xn2c = self.sbuf(ctxB, f"p7_x_{bi}", [128, DC, L], BF16)
                    Ksb = self.sbuf(ctxB, f"p7_K_{bi}", [128, DC, L], BF16)
                    Vau = self.sbuf(ctxB, f"p7_V_{bi}", [128, 16, NHEAD * 65], BF16)
                    Qs = self.sbuf(ctxB, f"p7_Q_{bi}", [128, 2, TCH], BF16)
                    tmp8 = self.sbuf(ctxB, f"p7_tmp8_{bi}", [64, 2, TCH], BF16)
                    Pt = self.sbuf(ctxB, f"p7_P_{bi}", [128, 16, TCH], BF16)
                    rL = self.sbuf(ctxB, f"p7_rL_{bi}", [128, TCH], F32)
                    rbc = self.sbuf(ctxB, f"p7_rbc_{bi}", [64, TCH], F32)
                    att = self.sbuf(ctxB, f"p7_att_{bi}", [128, DC, TCH], BF16)
                    aoc = Pt[:, 0:DC, :]
                    ps = self.psum(ctxB, f"p7_ps_{bi}", [128, 8, TCH])
                    P.dma("sp", xn2c[:], xn2T[:, :, tb:tb + L], writes=["xn2c"])
                    for ft in range(DC):
                        for cc in range(4):
                            sl = cc % 2
                            P.mm_group([(ps[:, sl, :],
                                         Wk[:, dc, ft * 128:(ft + 1) * 128],
                                         xn2c[:, dc, cc * TCH:(cc + 1) * TCH])
                                        for dc in range(DC)],
                                       writes=[f"b{sl}"], reads=["WkT", "xn2c"])
                            P.op("act", "activation",
                                 Ksb[:, ft, cc * TCH:(cc + 1) * TCH], ps[:, sl, :],
                                 AF.Identity, bias=bk[:, ft:ft + 1],
                                 reads=[f"b{sl}", "bk"], writes=["Ksb"])
                        P.op("dve", "tensor_add", Ksb[0:64, ft, :], Ksb[0:64, ft, :],
                             Erb[0:64, :], reads=["Ksb", "Erb"], writes=["Ksb"])
                        P.op("dve", "tensor_add", Ksb[64:128, ft, :], Ksb[64:128, ft, :],
                             Erb[64:128, :], reads=["Ksb", "Erb"], writes=["Ksb"])
                    Vh = Vau[:].rearrange("p t (h u) -> p t h u", u=65)
                    P.op("dve", "memset", Vh[:, :, :, 64:65], 1.0, writes=["Vau_one"])
                    for tt in range(16):
                        for i in range(2):
                            sl = 2 + (tt % 2)
                            pairs = [(ps[:, sl, :],
                                      xn2c[:, dc, tt * 128:(tt + 1) * 128],
                                      Wv[:, dc, i * 512:(i + 1) * 512])
                                     for dc in range(DC)]
                            pairs.append((ps[:, sl, :], on_b[0:1, 0:128],
                                          bvb[0:1, i * 512:(i + 1) * 512]))
                            P.mm_group(pairs, writes=[f"b{sl}"],
                                       reads=["xn2c", "WvT", "onb", "bvb"])
                            for hh in range(8):
                                h = i * 8 + hh
                                P.op("dve", "tensor_copy",
                                     Vau[:, tt, h * 65:h * 65 + 64],
                                     ps[:, sl, hh * 64:(hh + 1) * 64],
                                     reads=[f"b{sl}"], writes=["Vau"])
                    for qc in range(QC):
                        q0 = qc * TCH
                        for hp in range(8):
                            qp = hp % 2
                            qb = 4 if qp == 0 else 2
                            P.mm_group([(ps[:, qb, :],
                                         Wq[:, dc, hp * 128:(hp + 1) * 128],
                                         xn2c[:, dc, q0:q0 + TCH])
                                        for dc in range(DC)],
                                       writes=[f"b{qb}"], reads=["WqT", "xn2c"])
                            P.op("act", "activation", Qs[:, qp, :], ps[:, qb, :],
                                 AF.Identity, bias=bq[:, hp:hp + 1],
                                 reads=[f"b{qb}", "bq"], writes=[f"Qs{qp}"])
                            for ho in range(2):
                                h = 2 * hp + ho
                                hb = ho * 64
                                hr = h % 2
                                for kt in range(16):
                                    b0_ = 0 if (kt % 4) < 2 else 6
                                    sl = b0_ + (kt % 2)
                                    P.mm_group([(ps[:, sl, :],
                                                 Ksb[hb:hb + 64, hp,
                                                     kt * 128:(kt + 1) * 128],
                                                 Qs[hb:hb + 64, qp, :])],
                                               writes=[f"b{sl}"],
                                               reads=["Ksb", f"Qs{qp}"])
                                    if kt % 2 == 1:
                                        P.op("act", "activation",
                                             Pt[:, kt - 1:kt + 1, :],
                                             ps[:, b0_:b0_ + 2, :],
                                             AF.Exp, scale=0.125,
                                             reads=[f"b{b0_}", f"b{b0_ + 1}"],
                                             writes=["Pt"])
                                P.mm_group([(ps[0:65, 3, :],
                                             Vau[:, kt2, h * 65:h * 65 + 65],
                                             Pt[:, kt2, :]) for kt2 in range(16)],
                                           writes=["b3"], reads=["Vau", "Pt"])
                                P.op("dve", "reciprocal", rL[64:65, :],
                                     ps[64:65, 3, :], reads=["b3"], writes=["rL"])
                                P.mm_group([(ps[0:64, 5, :], on_f[64:65, 0:64],
                                             rL[64:65, :])],
                                           writes=["b5"], reads=["onf", "rL"])
                                P.op("act", "activation", rbc[0:64, :],
                                     ps[0:64, 5, :], AF.Identity,
                                     reads=["b5"], writes=["rbc"])
                                if ho == 0:
                                    P.op("dve", "tensor_mul", att[0:64, hp, :],
                                         ps[0:64, 3, :], rbc[0:64, :],
                                         reads=["b3", "rbc"], writes=["att"])
                                else:
                                    tp = hp % 2
                                    P.op("dve", "tensor_mul", tmp8[0:64, tp, :],
                                         ps[0:64, 3, :], rbc[0:64, :],
                                         reads=["b3", "rbc"], writes=[f"tmp8{tp}"])
                                    P.dma("sp", att[64:128, hp, :], tmp8[0:64, tp, :],
                                          reads=[f"tmp8{tp}"], writes=["att"])
                        for ft in range(DC):
                            sl = 6 + (ft % 2)
                            P.mm_group([(ps[:, sl, :],
                                         Wo[:, dc, ft * 128:(ft + 1) * 128],
                                         att[:, dc, :]) for dc in range(DC)],
                                       writes=[f"b{sl}"], reads=["WoT", "att"])
                            P.op("act", "activation", aoc[:, ft, :], ps[:, sl, :],
                                 AF.Identity, bias=bo[:, ft:ft + 1],
                                 reads=[f"b{sl}", "bo"], writes=["Pt"])
                        P.dma("sp", attoT[:, :, tb + q0:tb + q0 + TCH], aoc[:],
                              reads=["Pt"])
                    P.barrier_all()
            P.barrier_all()


import numpy as np
import ml_dtypes
from contextlib import ExitStack
from concourse.bass_utils import run_bass_kernel_spmd

BF = ml_dtypes.bfloat16
NP2BIR = {np.dtype(np.float32): F32, np.dtype(BF): BF16}
NCORES = 8

SCRATCH = {
    "xn1T": ((D, TOK), BF),
    "xn1fT": ((D, TOK), np.float32),
    "xn2fT": ((D, TOK), np.float32),
    "hT": ((D, TOK), BF),
    "sbrT": ((D, TOK), BF),
    "comb0T": ((8, TOK), np.float32),
    "comb1T": ((8, TOK), np.float32),
    "accT": ((D, TOK), np.float32),
    "x1T": ((D, TOK), np.float32),
    "xn2T": ((D, TOK), BF),
    "attoT": ((D, TOK), BF),
    "g2T": ((D, TOK), BF),
    "in_w_bf": ((128, 8, 6144), BF),
    "w1b": ((8, 128, 8, 2816), BF),
    "w3b": ((8, 128, 8, 2816), BF),
    "w2b": ((8, 128, 22, 1024), BF),
    "outT": ((D, TOK), np.float32),
}

PHASES = [
    ("phase_ln1", ["x0T", "ln0_gamma_t", "ln0_beta_t", "in_w", "w1T", "w3T",
                   "w2T"], [],
     ["xn1T", "xn1fT", "hT", "in_w_bf", "w1b", "w3b", "w2b"]),
    ("phase_mamba", ["in_b_t", "conv_w_t", "conv_b_t", "A_log_t",
                     "B_p_t", "C_p_t", "D_p_t", "out_w", "out_b_t", "identnk"],
     ["xn1T", "hT", "in_w_bf"], ["sbrT"]),
    ("phase_gate0", ["gate_w_T", "ident", "ones1_f32", "tiebk"],
     ["xn1fT"], ["comb0T"]),
    ("phase_moe0", ["sel"], ["xn1T", "comb0T", "w1b", "w3b", "w2b"], ["accT"]),
    ("phase_combine0", ["x0T"], ["sbrT", "accT"], ["x1T", "xn2T", "xn2fT"]),
    ("phase_gate1", ["gate_W", "gate_b_row", "ones1_f32", "ident", "tiebk"],
     ["xn2fT"], ["comb1T"]),
    ("phase_attn", ["WqT", "WkT", "WvT", "WoT", "bq_t", "bk_t", "bo_t",
                    "bv_row", "ErT"], ["xn2T"], ["attoT"]),
    ("phase_moe1", ["expert_W", "expert_b", "sel"], ["xn2T", "comb1T"], ["g2T"]),
    ("phase_final", [], ["x1T", "attoT", "g2T"], ["outT"]),
]


def tcol(vec, nparts=128):
    n = vec.shape[0]
    return np.ascontiguousarray(vec.reshape(n // nparts, nparts).T)


def host_inputs_core(inp, core):
    b0 = core * 2
    x0 = np.asarray(inp["L0_attn_x"])[b0:b0 + 2]
    x0T = np.ascontiguousarray(x0.reshape(TOK, D).T)
    sel = np.zeros((8, 1024), dtype=np.float32)
    for e in range(8):
        sel[e, e * 128:(e + 1) * 128] = 1.0
    m = {
        "x0T": x0T,
        "ln0_gamma_t": tcol(np.asarray(inp["L0_attn_norm_gamma"])),
        "ln0_beta_t": tcol(np.asarray(inp["L0_attn_norm_beta"])),
        "in_w": np.ascontiguousarray(np.asarray(inp["L0_attn_in_w"])),
        "in_b_t": tcol(np.asarray(inp["L0_attn_in_b"])),
        "conv_w_t": np.ascontiguousarray(
            np.asarray(inp["L0_attn_conv_w"]).reshape(2048, 3)
            .reshape(16, 128, 3).transpose(1, 0, 2)),
        "conv_b_t": tcol(np.asarray(inp["L0_attn_conv_b"])),
        "A_log_t": np.ascontiguousarray(
            np.asarray(inp["L0_attn_A_log"]).reshape(16, 128, 64)
            .transpose(1, 0, 2).reshape(128, 1024)),
        "B_p_t": np.ascontiguousarray(
            np.asarray(inp["L0_attn_B_p"]).reshape(16, 128, 64)
            .transpose(1, 0, 2).reshape(128, 1024)),
        "C_p_t": np.ascontiguousarray(
            np.asarray(inp["L0_attn_C_p"]).reshape(16, 128, 64)
            .transpose(1, 0, 2).reshape(128, 1024)),
        "D_p_t": tcol(np.asarray(inp["L0_attn_D_p"])),
        "out_w": np.ascontiguousarray(np.asarray(inp["L0_attn_out_w"])),
        "out_b_t": tcol(np.asarray(inp["L0_attn_out_b"])),
        "gate_w_T": np.ascontiguousarray(np.asarray(inp["L0_ffn_gate_w"]).T),
        "w1T": np.ascontiguousarray(np.asarray(inp["L0_ffn_w1"]).transpose(0, 2, 1)),
        "w3T": np.ascontiguousarray(np.asarray(inp["L0_ffn_w3"]).transpose(0, 2, 1)),
        "w2T": np.ascontiguousarray(np.asarray(inp["L0_ffn_w2"]).transpose(0, 2, 1)),
        "WqT": np.ascontiguousarray(np.asarray(inp["L1_attn_W_q"]).T),
        "WkT": np.ascontiguousarray(np.asarray(inp["L1_attn_W_k"]).T),
        "WvT": np.ascontiguousarray(np.asarray(inp["L1_attn_W_v"]).T),
        "WoT": np.ascontiguousarray(np.asarray(inp["L1_attn_W_o"]).T),
        "bq_t": tcol(np.asarray(inp["L1_attn_b_q"])),
        "bk_t": tcol(np.asarray(inp["L1_attn_b_k"])),
        "bo_t": tcol(np.asarray(inp["L1_attn_b_o"])),
        "bv_row": np.ascontiguousarray(np.asarray(inp["L1_attn_b_v"])[None, :]),
        "ErT": np.ascontiguousarray(
            np.asarray(inp["L1_attn_Er"])[2952:5000][::-1].T),
        "gate_W": np.ascontiguousarray(np.asarray(inp["L1_ffn_gate_W"])),
        "gate_b_row": np.ascontiguousarray(np.asarray(inp["L1_ffn_gate_b"])[None, :]),
        "expert_W": np.ascontiguousarray(np.asarray(inp["L1_ffn_expert_W"])),
        "expert_b": np.ascontiguousarray(np.asarray(inp["L1_ffn_expert_b"])),
        "ident": np.eye(128, dtype=np.float32),
        "identnk": np.broadcast_to(np.eye(128, dtype=np.float32).astype(
            ml_dtypes.bfloat16)[:, None, :], (128, 24, 128)),
        "ones1_f32": np.ones((1, 128), dtype=np.float32),
        "sel": sel,
        "tiebk": np.arange(7, -1, -1, dtype=np.float32)[None, :] * 1e-8,
    }
    return {k: np.ascontiguousarray(v) for k, v in m.items()}


def _build_phase(name, innames, sin, souts):
    import concourse.bass as bass
    nc = bass.Bass("TRN2", target_bir_lowering=False)
    io = {}
    for nm in innames:
        shp, dt = INPUT_SPECS[nm]
        io[nm] = nc.dram_tensor(nm, list(shp), NP2BIR[np.dtype(dt)],
                                kind="ExternalInput")
    for nm in sin:
        shp, dt = SCRATCH[nm]
        io[nm] = nc.dram_tensor(nm, list(shp), NP2BIR[np.dtype(dt)],
                                kind="ExternalInput")
    for nm in souts:
        shp, dt = SCRATCH[nm]
        io[nm] = nc.dram_tensor(nm, list(shp), NP2BIR[np.dtype(dt)],
                                kind="ExternalOutput")
    with ExitStack() as ctx:
        P = Prog(nc, ctx)
        getattr(MK(nc, P, io), name)()
    return nc


INPUT_SPECS = None
MODE = "fused"


def _build_fused():
    import concourse.bass as bass
    nc = bass.Bass("TRN2", target_bir_lowering=False)
    io = {}
    for nm, (shp, dt) in INPUT_SPECS.items():
        io[nm] = nc.dram_tensor(nm, list(shp), NP2BIR[np.dtype(dt)],
                                kind="ExternalInput")
    for nm, (shp, dt) in SCRATCH.items():
        kind = "ExternalOutput" if nm == "outT" else "Internal"
        io[nm] = nc.dram_tensor(nm, list(shp), NP2BIR[np.dtype(dt)], kind=kind)
    with ExitStack() as ctx:
        P = Prog(nc, ctx)
        mk = MK(nc, P, io)
        for name, _, _, _ in PHASES:
            getattr(mk, name)()
    return nc


def kernel(**inputs):
    global INPUT_SPECS
    maps = [host_inputs_core(inputs, c) for c in range(NCORES)]
    INPUT_SPECS = {k: (v.shape, v.dtype) for k, v in maps[0].items()}
    if MODE == "fused":
        nc = _build_fused()
        res = run_bass_kernel_spmd(nc, maps, core_ids=list(range(NCORES)))
        outs = [res.results[c]["outT"].T.reshape(2, L, D) for c in range(NCORES)]
        return np.ascontiguousarray(np.concatenate(outs, axis=0))
    state = [dict() for _ in range(NCORES)]
    for name, innames, sin, souts in PHASES:
        nc = _build_phase(name, innames, sin, souts)
        in_maps = []
        for c in range(NCORES):
            im = {k: maps[c][k] for k in innames}
            for k in sin:
                im[k] = state[c][k]
            in_maps.append(im)
        res = run_bass_kernel_spmd(nc, in_maps, core_ids=list(range(NCORES)))
        for c in range(NCORES):
            for k in souts:
                state[c][k] = res.results[c][k]
    outs = []
    for c in range(NCORES):
        outs.append(state[c]["outT"].T.reshape(2, L, D))
    return np.ascontiguousarray(np.concatenate(outs, axis=0))


# version 06
# speedup vs baseline: 1.2324x; 1.2324x over previous
import numpy as np
import concourse.bass as bass
import concourse.mybir as mybir

F32 = mybir.dt.float32
BF16 = mybir.dt.bfloat16
AF = mybir.ActivationFunctionType
ALU = mybir.AluOpType
AX = mybir.AxisListType

COMPUTE = ("pe", "dve", "act")
QUEUES = ("sp", "pl")
NLANES = 4


class Prog:
    def __init__(self, nc, ctx):
        self.nc = nc
        self.ctx = ctx
        names = list(COMPUTE) + [f"{q}{i}" for q in QUEUES for i in range(NLANES)]
        self.sem = {e: ctx.enter_context(nc.semaphore(f"ch_{e}")) for e in names}
        self.cnt = {e: 0 for e in names}
        self.lane_rr = {q: 0 for q in QUEUES}
        self.last_write = {}
        self.readers = {}
        self.hwm = {c: {p: 0 for p in names} for c in list(COMPUTE) + list(QUEUES)}
        self.names = names
        self.bg = ctx.enter_context(nc.semaphore("ch_bg"))
        self.bg_cnt = 0
        self.bg_hwm = {}
        self.bg_marks = {}

    def bg_dma(self, out_ap, in_ap):
        inst = self.nc.gpsimd.dma_start(out_ap, in_ap)
        self.bg_cnt += 16
        inst.then_inc(self.bg, 16)
        return self.bg_cnt

    def bg_wait(self, consumer, val):
        if val <= 0 or self.bg_hwm.get(consumer, 0) >= val:
            return
        self._engine_obj(consumer).wait_ge(self.bg, val)
        self.bg_hwm[consumer] = val

    def _engine_obj(self, e):
        nc = self.nc
        if e.startswith("sp"):
            return nc.sync
        if e.startswith("pl"):
            return nc.gpsimd
        return {"pe": nc.tensor, "dve": nc.vector, "act": nc.scalar}[e]

    def _need(self, consumer, producer, val):
        if val <= 0:
            return
        if self.hwm[consumer][producer] >= val:
            return
        self._engine_obj(consumer).wait_ge(self.sem[producer], val)
        self.hwm[consumer][producer] = val

    def pre(self, eng, reads=(), writes=()):
        for k in reads:
            lw = self.last_write.get(k)
            if lw is not None:
                self._need(eng, lw[0], lw[1])
        for k in writes:
            lw = self.last_write.get(k)
            if lw is not None:
                self._need(eng, lw[0], lw[1])
            for pe_, v in self.readers.get(k, {}).items():
                self._need(eng, pe_, v)

    def post(self, producer, consumer_key, inst, reads=(), writes=(), inc=1):
        self.cnt[producer] += inc
        c = self.cnt[producer]
        inst.then_inc(self.sem[producer], inc)
        for k in writes:
            self.last_write[k] = (producer, c)
            self.readers[k] = {}
        for k in reads:
            self.readers.setdefault(k, {})[producer] = c

    def op(self, eng, method, *args, reads=(), writes=(), inc=1, **kwargs):
        assert eng in COMPUTE
        self.pre(eng, reads, writes)
        inst = getattr(self._engine_obj(eng), method)(*args, **kwargs)
        self.post(eng, eng, inst, reads, writes, inc=inc)
        return inst

    def dma(self, queue, out_ap, in_ap, reads=(), writes=(), **kw):
        assert queue in QUEUES
        lane = f"{queue}{self.lane_rr[queue]}"
        self.lane_rr[queue] = (self.lane_rr[queue] + 1) % NLANES
        self._need(queue, lane, self.cnt[lane])
        self.pre(queue, reads, writes)
        inst = self._engine_obj(queue).dma_start(out_ap, in_ap, **kw)
        self.post(lane, queue, inst, reads, writes, inc=16)
        return inst

    def mm_group(self, pairs, writes, reads, accumulate=True, flags=None):
        self.pre("pe", reads, writes)
        n = len(pairs)
        inst = None
        for i, (out_ap, lhsT, rhs) in enumerate(pairs):
            if flags is not None:
                st, sp = flags[i]
            elif accumulate:
                st, sp = (i == 0), (i == n - 1)
            else:
                st, sp = True, True
            inst = self.nc.tensor.matmul(out_ap, lhsT, rhs, start=st, stop=sp)
        self.post("pe", "pe", inst, reads, writes, inc=1)
        return inst

    def barrier_all(self):
        for ce in list(COMPUTE) + list(QUEUES):
            for pe_ in self.names:
                if ce != pe_:
                    self._need(ce, pe_, self.cnt[pe_])




D = 1024
DC = 8
TOK = 4096
L = 2048
TCH = 512
NCH = TOK // TCH
INNER = 2048
CT = 16
HF = 2816
HT = 22
NK = 24
NHEAD = 16
DK = 64
EPS = 1e-5


class MK:
    def __init__(self, nc, P: Prog, io: dict):
        self.nc = nc
        self.P = P
        self.io = io

    def ap(self, name):
        return self.io[name].ap()

    def sbuf(self, ctx, name, shape, dtype):
        return ctx.enter_context(self.nc.sbuf_tensor(name, shape, dtype))

    def psum(self, ctx, name, shape, dtype=F32):
        return ctx.enter_context(self.nc.psum_tensor(name, shape, dtype))

    def _ln_chunk(self, b, src_f, kf, src_b, kb, dst, kdst, gam=None, bet=None,
                  kgam="gam", kbet="bet", dst2=None, kdst2=None):
        P = self.P
        ones, sq, psA, psB = b["ones"], b["sq"], b["psA"], b["psB"]
        mean, varr, msq, rstd, tmp = b["mean"], b["var"], b["msq"], b["rstd"], b["tmp"]
        P.op("act", "activation", sq[:], src_b[:], AF.Square, reads=[kb], writes=["sq"])
        P.mm_group([(psA[:], ones[:], src_b[:, dc, :]) for dc in range(DC)],
                   writes=["psA"], reads=[kb, "ones"])
        P.mm_group([(psB[:], ones[:], sq[:, dc, :]) for dc in range(DC)],
                   writes=["psB"], reads=["sq", "ones"])
        P.op("dve", "tensor_scalar_mul", mean[:], psA[:], 1.0 / D,
             reads=["psA"], writes=["mean"])
        P.op("dve", "tensor_scalar_mul", varr[:], psB[:], 1.0 / D,
             reads=["psB"], writes=["var"])
        P.op("dve", "tensor_mul", msq[:], mean[:], mean[:], reads=["mean"], writes=["msq"])
        P.op("dve", "tensor_sub", varr[:], varr[:], msq[:],
             reads=["var", "msq"], writes=["var"])
        P.op("act", "activation", rstd[:], varr[:], AF.Ln, bias=b["epsb"][:, 0:1],
             reads=["var", "epsb"], writes=["rstd"])
        P.op("act", "activation", rstd[:], rstd[:], AF.Exp, scale=-0.5,
             reads=["rstd"], writes=["rstd"])
        for dc in range(DC):
            P.op("dve", "tensor_sub", tmp[:, dc, :], src_f[:, dc, :], mean[:],
                 reads=[kf, "mean"], writes=["tmp"])
            if gam is None:
                if dst2 is not None:
                    P.op("dve", "tensor_mul", dst2[:, dc, :], tmp[:, dc, :], rstd[:],
                         reads=["tmp", "rstd"], writes=[kdst2])
                    P.op("dve", "tensor_copy", dst[:, dc, :], dst2[:, dc, :],
                         reads=[kdst2], writes=[kdst])
                else:
                    P.op("dve", "tensor_mul", dst[:, dc, :], tmp[:, dc, :], rstd[:],
                         reads=["tmp", "rstd"], writes=[kdst])
            else:
                P.op("dve", "tensor_mul", tmp[:, dc, :], tmp[:, dc, :], rstd[:],
                     reads=["tmp", "rstd"], writes=["tmp"])
                P.op("dve", "tensor_scalar", dst[:, dc, :], tmp[:, dc, :],
                     gam[:, dc:dc + 1], bet[:, dc:dc + 1], ALU.mult, ALU.add,
                     reads=["tmp", kgam, kbet], writes=[kdst])

    def _ln_bufs(self, ctx, pref):
        b = {}
        b["ones"] = self.sbuf(ctx, pref + "ones", [128, 128], BF16)
        b["sq"] = self.sbuf(ctx, pref + "sq", [128, DC, TCH], BF16)
        b["mean"] = self.sbuf(ctx, pref + "mean", [128, TCH], F32)
        b["var"] = self.sbuf(ctx, pref + "var", [128, TCH], F32)
        b["msq"] = self.sbuf(ctx, pref + "msq", [128, TCH], F32)
        b["rstd"] = self.sbuf(ctx, pref + "rstd", [128, TCH], F32)
        b["tmp"] = self.sbuf(ctx, pref + "tmp", [128, DC, TCH], F32)
        b["epsb"] = self.sbuf(ctx, pref + "epsb", [128, 1], F32)
        b["psA"] = self.psum(ctx, pref + "psA", [128, TCH])
        b["psB"] = self.psum(ctx, pref + "psB", [128, TCH])
        self.P.op("dve", "memset", b["ones"][:], 1.0, writes=["ones"])
        self.P.op("dve", "memset", b["epsb"][:], EPS, writes=["epsb"])
        return b

    def phase_ln1(self):
        nc, P = self.nc, self.P
        with ExitStack() as ctx:
            b = self._ln_bufs(ctx, "p1_")
            gam = self.sbuf(ctx, "p1_gam", [128, DC], F32)
            bet = self.sbuf(ctx, "p1_bet", [128, DC], F32)
            x0c = self.sbuf(ctx, "p1_x0c", [128, DC, TCH], F32)
            x0b = self.sbuf(ctx, "p1_x0b", [128, DC, TCH], BF16)
            xn1c = self.sbuf(ctx, "p1_xn1c", [128, DC, TCH], BF16)
            xn1f = self.sbuf(ctx, "p1_xn1f", [128, DC, TCH], F32)
            hc = self.sbuf(ctx, "p1_hc", [128, DC, TCH], BF16)
            P.dma("sp", gam[:], self.ap("ln0_gamma_t"), writes=["gam"])
            P.dma("sp", bet[:], self.ap("ln0_beta_t"), writes=["bet"])
            P.bg_marks = getattr(P, "bg_marks", {})
            P.bg_marks["in_w_bf"] = P.bg_dma(
                self.ap("in_w_bf"),
                self.ap("in_w").rearrange("(c p) f -> p c f", p=128))
            for e in range(8):
                P.bg_dma(self.ap("w1b")[e],
                         self.ap("w1T")[e].rearrange("(c p) h -> p c h", p=128))
                P.bg_dma(self.ap("w3b")[e],
                         self.ap("w3T")[e].rearrange("(c p) h -> p c h", p=128))
                P.bg_marks[("w", e)] = P.bg_dma(
                    self.ap("w2b")[e],
                    self.ap("w2T")[e].rearrange("(t p) f -> p t f", p=128))
            x0T = self.ap("x0T").rearrange("(c p) t -> p c t", p=128)
            xn1T = self.ap("xn1T").rearrange("(c p) t -> p c t", p=128)
            xn1fT = self.ap("xn1fT").rearrange("(c p) t -> p c t", p=128)
            hT = self.ap("hT").rearrange("(c p) t -> p c t", p=128)
            for ch in range(NCH):
                t0 = ch * TCH
                P.dma("sp", x0c[:], x0T[:, :, t0:t0 + TCH], writes=["x0c"])
                P.op("act", "activation", x0b[:], x0c[:], AF.Copy,
                     reads=["x0c"], writes=["x0b"])
                self._ln_chunk(b, x0c, "x0c", x0b, "x0b", xn1c, "xn1c",
                               dst2=xn1f, kdst2="xn1f")
                P.dma("sp", xn1T[:, :, t0:t0 + TCH], xn1c[:], reads=["xn1c"])
                P.dma("sp", xn1fT[:, :, t0:t0 + TCH], xn1f[:], reads=["xn1f"])
                self._ln_chunk(b, xn1c, "xn1c", xn1c, "xn1c", hc, "hc",
                               gam=gam, bet=bet)
                P.dma("sp", hT[:, :, t0:t0 + TCH], hc[:], reads=["hc"])
            P.barrier_all()

    def phase_mamba(self):
        nc, P = self.nc, self.P
        hT = self.ap("hT").rearrange("(c p) t -> p c t", p=128)
        xn1T = self.ap("xn1T").rearrange("(c p) t -> p c t", p=128)
        sbrT = self.ap("sbrT").rearrange("(c p) t -> p c t", p=128)
        with ExitStack() as ctx:
            us = self.sbuf(ctx, "p2_us", [128, CT, 32 + L], BF16)

            K_sb = self.sbuf(ctx, "p2_K", [128, CT, NK], F32)
            Kb = self.sbuf(ctx, "p2_Kb", [128, CT, NK], BF16)
            idnk = self.sbuf(ctx, "p2_idnk", [128, NK, 128], BF16)
            cw = self.sbuf(ctx, "p2_cw", [128, CT, 3], BF16)
            cb = self.sbuf(ctx, "p2_cb", [128, CT], F32)
            inb = self.sbuf(ctx, "p2_inb", [128, 48], F32)
            tgb = self.sbuf(ctx, "p2_tgb", [128, CT], F32)
            Dp = self.sbuf(ctx, "p2_Dp", [128, CT], F32)
            outb = self.sbuf(ctx, "p2_outb", [128, DC], F32)

            P.dma("sp", idnk[:], self.ap("identnk"), writes=["idnk"])
            kctx = ExitStack()
            eA = self.sbuf(kctx, "p2_eA", [128, 1024], F32)
            R = self.sbuf(kctx, "p2_R", [128, 1024], F32)
            Cp = self.sbuf(kctx, "p2_C", [128, 1024], F32)
            P.dma("sp", eA[:], self.ap("A_log_t"), writes=["eA"])
            P.dma("sp", R[:], self.ap("B_p_t"), writes=["R"])
            P.dma("sp", Cp[:], self.ap("C_p_t"), writes=["Cp"])
            P.dma("pl", cw[:], self.ap("conv_w_t"), writes=["cw"])
            P.dma("sp", cb[:], self.ap("conv_b_t"), writes=["cb"])
            P.dma("sp", inb[:], self.ap("in_b_t"), writes=["inb"])
            P.dma("sp", Dp[:], self.ap("D_p_t"), writes=["Dp"])
            P.dma("sp", outb[:], self.ap("out_b_t"), writes=["outb"])

            P.op("act", "activation", eA[:], eA[:], AF.Exp, reads=["eA"], writes=["eA"])
            P.op("act", "activation", eA[:], eA[:], AF.Exp, scale=-1.0,
                 reads=["eA"], writes=["eA"])
            P.op("dve", "tensor_mul", R[:], R[:], Cp[:], reads=["R", "Cp"], writes=["R"])
            Rv = R[:].rearrange("p (c n) -> p c n", n=64)
            for d in range(NK):
                if d > 0:
                    P.op("dve", "tensor_mul", R[:], R[:], eA[:], reads=["R", "eA"],
                         writes=["R"])
                P.op("dve", "tensor_reduce", K_sb[:, :, d], Rv, axis=AX.X, op=ALU.add,
                     reads=["R"], writes=["K"])
            P.op("dve", "tensor_add", K_sb[:, :, 0], K_sb[:, :, 0], Dp[:],
                 reads=["K", "Dp"], writes=["K"])
            P.op("dve", "tensor_copy", Kb[:], K_sb[:], reads=["K"], writes=["Kb"])
            P.op("dve", "tensor_scalar_mul", tgb[:], inb[:, 32:48], 0.5,
                 reads=["inb"], writes=["tgb"])
            P.barrier_all()
            kctx.close()

            for bi in range(2):
                tb = bi * L
                with ExitStack() as ctxA:
                    u_raw = self.sbuf(ctxA, f"p2_uraw_{bi}", [128, CT, L + 2], BF16)
                    hc = self.sbuf(ctxA, f"p2_hcA_{bi}", [128, DC, TCH], BF16)
                    iw = self.sbuf(ctxA, f"p2_iwA_{bi}", [128, DC, 2048], BF16)
                    d3 = self.sbuf(ctxA, f"p2_d3_{bi}", [128, 2, 3, 128], BF16)
                    psA = self.psum(ctxA, f"p2_psA_{bi}", [128, 2, TCH])
                    psC = self.psum(ctxA, f"p2_psC_{bi}", [128, 2, TCH])
                    P.bg_wait("sp", P.bg_marks.get("in_w_bf", 0))
                    P.dma("sp", iw[:], self.ap("in_w_bf")[:, :, 0:2048],
                          writes=["iwA"])
                    P.op("dve", "memset", u_raw[:, :, 0:1], 0.0, writes=["uraw_pad"])
                    P.op("dve", "memset", u_raw[:, :, L + 1:L + 2], 0.0, writes=["uraw_pad"])
                    P.op("dve", "memset", us[:, :, 0:32], 0.0, writes=["us_pad"])
                    for cc in range(4):
                        t0 = cc * TCH
                        P.dma("sp", hc[:], hT[:, :, tb + t0: tb + t0 + TCH], writes=["hc"])
                        for ft in range(16):
                            bsl = ft % 2
                            P.mm_group([(psA[:, bsl, :],
                                         iw[:, dc, ft * 128:(ft + 1) * 128],
                                         hc[:, dc, :]) for dc in range(DC)],
                                       writes=[f"psA{bsl}"], reads=["iwA", "hc"])
                            P.op("act", "activation",
                                 u_raw[:, ft, 1 + t0:1 + t0 + TCH], psA[:, bsl, :],
                                 AF.Identity, bias=inb[:, ft:ft + 1],
                                 reads=[f"psA{bsl}", "inb"], writes=["uraw"])
                    for cc in range(4):
                        t0 = cc * TCH
                        for ct in range(CT):
                            r = ct % 2
                            P.op("dve", "tensor_tensor", d3[:, r], idnk[:, 0:3, :],
                                 cw[:, ct, :].unsqueeze(2).broadcast_to((128, 3, 128)),
                                 ALU.mult, reads=["idnk", "cw"], writes=[f"d3{r}"])
                            P.mm_group([(psC[:, r, :], d3[:, r, j, :],
                                         u_raw[:, ct, t0 + j:t0 + j + TCH])
                                        for j in range(3)],
                                       writes=[f"psC{r}"], reads=[f"d3{r}", "uraw"])
                            P.op("act", "activation", us[:, ct, 32 + t0:32 + t0 + TCH],
                                 psC[:, r, :], AF.Silu, bias=cb[:, ct:ct + 1],
                                 reads=[f"psC{r}", "cb"], writes=["us"])
                    if getattr(self, "mamba_dbg", None) == "us" and bi == 0:
                        P.dma("sp", self.io["dbgT"].ap(), us[:].rearrange("p c t -> p (c t)"), reads=["us"])
                    P.barrier_all()
                with ExitStack() as ctxB:
                    out_w_sb = self.sbuf(ctxB, f"p2_outw_{bi}", [128, CT, 1024], BF16)
                    hc = self.sbuf(ctxB, f"p2_hcB_{bi}", [128, DC, TCH], BF16)
                    xc = self.sbuf(ctxB, f"p2_xc_{bi}", [128, DC, TCH], BF16)
                    iw = self.sbuf(ctxB, f"p2_iwB_{bi}", [128, 2, DC, 256], BF16)
                    P.dma("pl", out_w_sb[:],
                          self.ap("out_w").rearrange("(ct p) f -> p ct f", p=128),
                          writes=["outw"])
                    sv = self.sbuf(ctxB, f"p2_sv_{bi}", [128, CT, TCH], BF16)
                    th = self.sbuf(ctxB, f"p2_th_{bi}", [128, CT, TCH], BF16)
                    dg = self.sbuf(ctxB, f"p2_dg_{bi}", [128, 2, NK, 128], BF16)
                    ym = self.sbuf(ctxB, f"p2_ym_{bi}", [128, CT, TCH], BF16)
                    t1 = self.sbuf(ctxB, f"p2_t1_{bi}", [128, TCH], F32)
                    sbc = self.sbuf(ctxB, f"p2_sbc_{bi}", [128, DC, TCH], BF16)
                    psB = self.psum(ctxB, f"p2_psB_{bi}", [128, 2, TCH])
                    psY = self.psum(ctxB, f"p2_psY_{bi}", [128, 2, TCH])
                    psM = self.psum(ctxB, f"p2_psM_{bi}", [128, 4, TCH])
                    for cc in range(4):
                        t0 = cc * TCH
                        P.dma("sp", hc[:], hT[:, :, tb + t0: tb + t0 + TCH], writes=["hc"])
                        P.dma("sp", xc[:], xn1T[:, :, tb + t0: tb + t0 + TCH], writes=["xc"])
                        for fb in range(16):
                            sl = fb % 2
                            P.dma("sp", iw[:, sl],
                                  self.ap("in_w_bf")[:, :, 2048 + fb * 256:
                                                     2048 + (fb + 1) * 256],
                                  writes=[f"iw{sl}"])
                            for j in range(2):
                                ft = 16 + fb * 2 + j
                                bsl = ft % 2
                                P.mm_group([(psB[:, bsl, :],
                                             iw[:, sl, dc, j * 128:(j + 1) * 128],
                                             hc[:, dc, :]) for dc in range(DC)],
                                           writes=[f"psB{bsl}"], reads=[f"iw{sl}", "hc"])
                                if ft < 32:
                                    P.op("act", "activation", sv[:, ft - 16, :],
                                         psB[:, bsl, :], AF.Silu, bias=inb[:, ft:ft + 1],
                                         reads=[f"psB{bsl}", "inb"], writes=["sv"])
                                else:
                                    P.op("act", "activation", th[:, ft - 32, :],
                                         psB[:, bsl, :], AF.Tanh, scale=0.5,
                                         bias=tgb[:, ft - 32:ft - 31],
                                         reads=[f"psB{bsl}", "tgb"], writes=["th"])
                        for ct in range(CT):
                            ysl = ct % 2
                            P.op("dve", "tensor_tensor", dg[:, ysl], idnk[:],
                                 Kb[:, ct, :].unsqueeze(2).broadcast_to((128, NK, 128)),
                                 ALU.mult, reads=["idnk", "Kb"], writes=[f"dg{ysl}"])
                            P.mm_group([(psY[:, ysl, :], dg[:, ysl, d, :],
                                         us[:, ct, 32 + t0 - d:32 + t0 - d + TCH])
                                        for d in range(NK)],
                                       writes=[f"psY{ysl}"], reads=[f"dg{ysl}", "us"])
                            P.op("dve", "tensor_mul", t1[:], psY[:, ysl, :], th[:, ct, :],
                                 reads=[f"psY{ysl}", "th"], writes=["t1"])
                            P.op("dve", "tensor_add", t1[:], t1[:], psY[:, ysl, :],
                                 reads=["t1", f"psY{ysl}"], writes=["t1"])
                            P.op("dve", "scalar_tensor_tensor", ym[:, ct, :], t1[:], 0.5,
                                 sv[:, ct, :], ALU.mult, ALU.add,
                                 reads=["t1", "sv"], writes=["ym"])
                        for fh in range(2):
                            for f in range(4):
                                fo = fh * 4 + f
                                P.mm_group([(psM[:, f, :],
                                             out_w_sb[:, ct2, fo * 128:(fo + 1) * 128],
                                             ym[:, ct2, :]) for ct2 in range(CT)],
                                           writes=[f"psM{f}"], reads=["outw", "ym"])
                            for f in range(4):
                                fo = fh * 4 + f
                                P.op("dve", "scalar_tensor_tensor", sbc[:, fo, :],
                                     psM[:, f, :], outb[:, fo:fo + 1], xc[:, fo, :],
                                     ALU.add, ALU.add,
                                     reads=[f"psM{f}", "outb", "xc"], writes=["sbc"])
                        P.dma("sp", sbrT[:, :, tb + t0: tb + t0 + TCH], sbc[:],
                              reads=["sbc"])
                    P.barrier_all()
            P.barrier_all()

    def _gate_phase(self, which):
        nc, P = self.nc, self.P
        srcT = self.ap("xn1fT" if which == 0 else "xn2fT").rearrange("(c p) t -> p c t", p=128)
        dstT = self.ap("comb0T" if which == 0 else "comb1T")
        gwname = "gate_w_T" if which == 0 else "gate_W"
        pr = f"g{which}_"
        with ExitStack() as ctx:
            gw = self.sbuf(ctx, pr + "gw", [128, DC, 8], F32)
            xc = self.sbuf(ctx, pr + "xc", [128, DC, TCH], F32)
            ident = self.sbuf(ctx, pr + "id", [128, 128], F32)
            cT = self.sbuf(ctx, pr + "cT", [8, TOK], F32)
            sc = self.sbuf(ctx, pr + "sc", [128, 8], F32)
            mx = self.sbuf(ctx, pr + "mx", [128, 1], F32)
            p = self.sbuf(ctx, pr + "p", [128, 8], F32)
            m1 = self.sbuf(ctx, pr + "m1", [128, 8], F32)
            m2 = self.sbuf(ctx, pr + "m2", [128, 8], F32)
            sm = self.sbuf(ctx, pr + "sm", [128, 1], F32)
            psG = self.psum(ctx, pr + "psG", [128, 8])
            psT = self.psum(ctx, pr + "psT", [8, 128])
            P.dma("sp", gw[:], self.ap(gwname).rearrange("(c p) e -> p c e", p=128),
                  writes=["gw"])
            P.dma("sp", ident[:], self.ap("ident"), writes=["ident"])
            on1 = self.sbuf(ctx, pr + "on1", [1, 128], F32)
            tbr = self.sbuf(ctx, pr + "tbr", [1, 8], F32)
            tbb = self.sbuf(ctx, pr + "tbb", [128, 8], F32)
            ptb = self.sbuf(ctx, pr + "ptb", [128, 8], F32)
            psBB = self.psum(ctx, pr + "psBB", [128, 8])
            P.dma("sp", on1[:], self.ap("ones1_f32"), writes=["on1"])
            P.dma("sp", tbr[:], self.ap("tiebk"), writes=["tbr"])
            P.mm_group([(psBB[:], on1[:], tbr[:])], writes=["psBB"],
                       reads=["on1", "tbr"])
            P.op("act", "activation", tbb[:], psBB[:], AF.Identity,
                 reads=["psBB"], writes=["tbb"])
            if which == 1:
                gbr = self.sbuf(ctx, pr + "gbr", [1, 8], F32)
                gbb = self.sbuf(ctx, pr + "gbb", [128, 8], F32)
                P.dma("sp", gbr[:], self.ap("gate_b_row"), writes=["gbr"])
                P.mm_group([(psBB[:], on1[:], gbr[:])], writes=["psBB"],
                           reads=["on1", "gbr"])
                P.op("act", "activation", gbb[:], psBB[:], AF.Identity,
                     reads=["psBB"], writes=["gbb"])
            for ch in range(NCH):
                t0 = ch * TCH
                P.dma("sp", xc[:], srcT[:, :, t0:t0 + TCH], writes=["xc"])
                for tt in range(TCH // 128):
                    tok0 = t0 + tt * 128
                    P.mm_group([(psG[:], xc[:, dc, tt * 128:(tt + 1) * 128], gw[:, dc, :])
                                for dc in range(DC)], writes=["psG"], reads=["xc", "gw"])
                    if which == 0:
                        P.op("act", "activation", sc[:], psG[:], AF.Identity,
                             reads=["psG"], writes=["sc"])
                    else:
                        P.op("dve", "tensor_add", sc[:], psG[:], gbb[:],
                             reads=["psG", "gbb"], writes=["sc"])
                        P.op("dve", "tensor_scalar_max", sc[:], sc[:], 1e-4,
                             reads=["sc"], writes=["sc"])
                    P.op("dve", "reduce_max", mx[:, 0:1], sc[:], axis=AX.X,
                         reads=["sc"], writes=["mx"])
                    P.op("dve", "tensor_scalar", p[:], sc[:], mx[:, 0:1], None,
                         ALU.subtract, reads=["sc", "mx"], writes=["p"])
                    P.op("act", "activation", p[:], p[:], AF.Exp,
                         reads=["p"], writes=["p"])
                    P.op("dve", "reduce_sum", sm[:, 0:1], p[:], axis=AX.X,
                         reads=["p"], writes=["sm"])
                    P.op("dve", "reciprocal", sm[:, 0:1], sm[:, 0:1],
                         reads=["sm"], writes=["sm"])
                    P.op("dve", "tensor_scalar_mul", p[:], p[:], sm[:, 0:1],
                         reads=["p", "sm"], writes=["p"])
                    P.op("dve", "tensor_add", ptb[:], p[:], tbb[:],
                         reads=["p", "tbb"], writes=["ptb"])
                    P.op("dve", "reduce_max", mx[:, 0:1], ptb[:], axis=AX.X,
                         reads=["ptb"], writes=["mx"])
                    P.op("dve", "tensor_scalar", m1[:], ptb[:], mx[:, 0:1], None,
                         ALU.is_ge, reads=["ptb", "mx"], writes=["m1"])
                    P.op("dve", "scalar_tensor_tensor", m2[:], m1[:], -1e30, ptb[:],
                         ALU.mult, ALU.add, reads=["m1", "ptb"], writes=["m2"])
                    P.op("dve", "reduce_max", mx[:, 0:1], m2[:], axis=AX.X,
                         reads=["m2"], writes=["mx"])
                    P.op("dve", "tensor_scalar", m2[:], m2[:], mx[:, 0:1], None,
                         ALU.is_ge, reads=["m2", "mx"], writes=["m2"])
                    P.op("dve", "tensor_max", m1[:], m1[:], m2[:],
                         reads=["m1", "m2"], writes=["m1"])
                    if which == 1:
                        P.op("dve", "tensor_scalar_max", p[:], p[:], 0.1,
                             reads=["p"], writes=["p"])
                    P.op("dve", "tensor_mul", p[:], p[:], m1[:],
                         reads=["p", "m1"], writes=["p"])
                    P.op("dve", "reduce_sum", sm[:, 0:1], p[:], axis=AX.X,
                         reads=["p"], writes=["sm"])
                    P.op("dve", "reciprocal", sm[:, 0:1], sm[:, 0:1],
                         reads=["sm"], writes=["sm"])
                    if which == 1:
                        P.op("dve", "tensor_scalar_mul", sm[:, 0:1], sm[:, 0:1], 0.5,
                             reads=["sm"], writes=["sm"])
                    P.op("dve", "tensor_scalar_mul", p[:], p[:], sm[:, 0:1],
                         reads=["p", "sm"], writes=["p"])
                    dbg = getattr(self, "gate_dbg", None)
                    if dbg == "sc":
                        P.op("dve", "tensor_copy", p[:], sc[:], reads=["sc"], writes=["p"])
                    elif dbg == "m1":
                        P.op("dve", "tensor_copy", p[:], m1[:], reads=["m1"], writes=["p"])
                    P.op("pe", "transpose", psT[:, :], p[:], ident[:],
                         reads=["p", "ident"], writes=["psT"])
                    P.op("dve", "tensor_copy", cT[:, tok0:tok0 + 128], psT[0:8, :],
                         reads=["psT"], writes=["cT"])
            P.dma("sp", dstT, cT[:], reads=["cT"])
            P.barrier_all()

    def phase_gate0(self):
        self._gate_phase(0)

    def phase_gate1(self):
        self._gate_phase(1)

    def phase_moe0(self):
        nc, P = self.nc, self.P
        xn1T = self.ap("xn1T").rearrange("(c p) t -> p c t", p=128)
        accT = self.ap("accT").rearrange("(c p) t -> p c t", p=128)
        with ExitStack() as ctx:
            sel = self.sbuf(ctx, "p4_sel", [8, 1024], F32)
            cT = self.sbuf(ctx, "p4_cT", [8, 2, TCH], F32)
            P.dma("sp", sel[:], self.ap("sel"), writes=["sel"])
            NE = getattr(self, "moe0_experts", 8)
            NCHX = getattr(self, "moe0_chunks", NCH)
            for e in range(NE):
                with ExitStack() as ctxE:
                    w1s = self.sbuf(ctxE, f"p4_w1_{e}", [128, DC, HF], BF16)
                    w3s = self.sbuf(ctxE, f"p4_w3_{e}", [128, DC, HF], BF16)
                    w2s = self.sbuf(ctxE, f"p4_w2_{e}", [128, HT, 1024], BF16)
                    xc = self.sbuf(ctxE, f"p4_xc_{e}", [128, 2, DC, TCH], BF16)
                    cbc = self.sbuf(ctxE, f"p4_cbc_{e}", [128, 2, TCH], F32)
                    h1s = self.sbuf(ctxE, f"p4_h1s_{e}", [128, 2, TCH], BF16)
                    h3s = self.sbuf(ctxE, f"p4_h3s_{e}", [128, 2, TCH], BF16)
                    H = self.sbuf(ctxE, f"p4_H_{e}", [128, HT, TCH], BF16)
                    accc = self.sbuf(ctxE, f"p4_acc_{e}", [128, DC, TCH], F32)
                    ps8 = self.psum(ctxE, f"p4_ps_{e}", [128, 8, TCH])
                    P.bg_wait("sp", P.bg_marks.get(("w", e), 0))
                    P.dma("sp", w1s[:], self.ap("w1b")[e], writes=["w1s"])
                    P.dma("sp", w3s[:], self.ap("w3b")[e], writes=["w3s"])
                    P.dma("sp", w2s[:], self.ap("w2b")[e], writes=["w2s"])
                    for ch in range(NCHX):
                        t0 = ch * TCH
                        par = ch % 2
                        P.dma("sp", xc[:, par], xn1T[:, :, t0:t0 + TCH],
                              writes=[f"xc{par}"])
                        P.dma("sp", cT[:, par, :], self.ap("comb0T")[:, t0:t0 + TCH],
                              writes=[f"cT{par}"])
                        P.mm_group([(ps8[:, 4, :], sel[0:8, e * 128:(e + 1) * 128],
                                     cT[0:8, par, :])],
                                   writes=["b4"], reads=["sel", f"cT{par}"])
                        P.op("act", "activation", cbc[:, par, :], ps8[:, 4, :], AF.Identity,
                             reads=["b4"], writes=[f"cbc{par}"])
                        for ht in range(HT):
                            r = ht % 2
                            ba, bb = 2 * r, 2 * r + 1
                            P.mm_group([(ps8[:, ba, :], w1s[:, dc, ht * 128:(ht + 1) * 128],
                                         xc[:, par, dc, :]) for dc in range(DC)],
                                       writes=[f"b{ba}"], reads=["w1s", f"xc{par}"])
                            P.mm_group([(ps8[:, bb, :], w3s[:, dc, ht * 128:(ht + 1) * 128],
                                         xc[:, par, dc, :]) for dc in range(DC)],
                                       writes=[f"b{bb}"], reads=["w3s", f"xc{par}"])
                            P.op("act", "activation", h1s[:, r, :], ps8[:, ba, :], AF.Silu,
                                 reads=[f"b{ba}"], writes=[f"h1s{r}"])
                            P.op("dve", "tensor_mul", h3s[:, r, :], ps8[:, bb, :],
                                 cbc[:, par, :],
                                 reads=[f"b{bb}", f"cbc{par}"], writes=[f"h3s{r}"])
                            P.op("dve", "tensor_mul", H[:, ht, :], h1s[:, r, :], h3s[:, r, :],
                                 reads=[f"h1s{r}", f"h3s{r}"], writes=["H"])
                        for f in range(8):
                            P.mm_group([(ps8[:, f, :], w2s[:, ht2, f * 128:(f + 1) * 128],
                                         H[:, ht2, :]) for ht2 in range(HT)],
                                       writes=[f"b{f}"], reads=["w2s", "H"])
                        for f in range(8):
                            eng = "act" if f % 2 == 0 else "dve"
                            meth = ("activation", ps8[:, f, :], AF.Identity) if eng == "act" else None
                            if eng == "act":
                                P.op("act", "activation", accc[:, f, :], ps8[:, f, :],
                                     AF.Identity, reads=[f"b{f}"], writes=["accc"])
                            else:
                                P.op("dve", "tensor_copy", accc[:, f, :], ps8[:, f, :],
                                     reads=[f"b{f}"], writes=["accc"])
                        if e == 0:
                            P.dma("pl", accT[:, :, t0:t0 + TCH], accc[:],
                                  reads=["accc"])
                        else:
                            P.dma("pl", accT[:, :, t0:t0 + TCH], accc[:],
                                  reads=["accc"], accum_op=ALU.add)
                    P.barrier_all()
            P.barrier_all()

    def phase_combine0(self):
        nc, P = self.nc, self.P
        x0T = self.ap("x0T").rearrange("(c p) t -> p c t", p=128)
        sbrT = self.ap("sbrT").rearrange("(c p) t -> p c t", p=128)
        accT = self.ap("accT").rearrange("(c p) t -> p c t", p=128)
        x1T = self.ap("x1T").rearrange("(c p) t -> p c t", p=128)
        xn2T = self.ap("xn2T").rearrange("(c p) t -> p c t", p=128)
        xn2fT = self.ap("xn2fT").rearrange("(c p) t -> p c t", p=128)
        with ExitStack() as ctx:
            x0c = self.sbuf(ctx, "p5_x0c", [128, DC, TCH], F32)
            sbc = self.sbuf(ctx, "p5_sbc", [128, DC, TCH], BF16)
            acc = self.sbuf(ctx, "p5_acc", [128, DC, TCH], F32)
            g = self.sbuf(ctx, "p5_g", [128, DC, TCH], F32)
            x1c = self.sbuf(ctx, "p5_x1c", [128, DC, TCH], F32)
            for ch in range(NCH):
                t0 = ch * TCH
                P.dma("sp", x0c[:], x0T[:, :, t0:t0 + TCH], writes=["x0c"])
                P.dma("sp", sbc[:], sbrT[:, :, t0:t0 + TCH], writes=["sbc"])
                P.dma("sp", acc[:], accT[:, :, t0:t0 + TCH], writes=["acc"])
                P.op("act", "activation", g[:], acc[:], AF.Silu,
                     reads=["acc"], writes=["g"])
                P.op("dve", "tensor_mul", g[:], g[:], sbc[:],
                     reads=["g", "sbc"], writes=["g"])
                P.op("dve", "tensor_add", x1c[:], x0c[:], g[:],
                     reads=["x0c", "g"], writes=["x1c"])
                P.dma("sp", x1T[:, :, t0:t0 + TCH], x1c[:], reads=["x1c"])
            P.barrier_all()
        with ExitStack() as ctx:
            b = self._ln_bufs(ctx, "p5b_")
            x1c = self.sbuf(ctx, "p5b_x1c", [128, DC, TCH], F32)
            x1b = self.sbuf(ctx, "p5b_x1b", [128, DC, TCH], BF16)
            xn2c = self.sbuf(ctx, "p5b_xn2c", [128, DC, TCH], BF16)
            xn2f = self.sbuf(ctx, "p5b_xn2f", [128, DC, TCH], F32)
            for ch in range(NCH):
                t0 = ch * TCH
                P.dma("sp", x1c[:], x1T[:, :, t0:t0 + TCH], writes=["x1c2"])
                P.op("act", "activation", x1b[:], x1c[:], AF.Copy,
                     reads=["x1c2"], writes=["x1b"])
                self._ln_chunk(b, x1c, "x1c2", x1b, "x1b", xn2c, "xn2c",
                               dst2=xn2f, kdst2="xn2f")
                P.dma("sp", xn2T[:, :, t0:t0 + TCH], xn2c[:], reads=["xn2c"])
                P.dma("sp", xn2fT[:, :, t0:t0 + TCH], xn2f[:], reads=["xn2f"])
            P.barrier_all()

    def phase_moe1(self):
        nc, P = self.nc, self.P
        xn2T = self.ap("xn2T").rearrange("(c p) t -> p c t", p=128)
        g2T = self.ap("g2T").rearrange("(c p) t -> p c t", p=128)
        with ExitStack() as ctx:
            ew = self.sbuf(ctx, "p8_ew", [128, 8, DC, 1024], BF16)
            eb = self.sbuf(ctx, "p8_eb", [8, 1024], BF16)
            sel = self.sbuf(ctx, "p8_sel", [8, 1024], F32)
            cT = self.sbuf(ctx, "p8_cT", [8, TOK], F32)
            cTb = self.sbuf(ctx, "p8_cTb", [8, TOK], BF16)
            xc = self.sbuf(ctx, "p8_xc", [128, DC, TCH], BF16)
            cbc = self.sbuf(ctx, "p8_cbc", [128, 8, TCH], F32)
            xe = self.sbuf(ctx, "p8_xe", [128, 2, DC, TCH], BF16)
            g2c = self.sbuf(ctx, "p8_g2c", [128, DC, TCH], BF16)
            ps8 = self.psum(ctx, "p8_ps", [128, 8, TCH])
            P.dma("pl", ew[:], self.ap("expert_W").rearrange("e (c p) f -> p e c f", p=128),
                  writes=["ew"])
            P.dma("pl", eb[:], self.ap("expert_b"), writes=["eb"])
            P.dma("sp", sel[:], self.ap("sel"), writes=["sel"])
            P.dma("sp", cT[:], self.ap("comb1T"), writes=["cT"])
            P.dma("pl", cTb[:], self.ap("comb1T"), writes=["cTb"])
            for ch in range(NCH):
                t0 = ch * TCH
                P.dma("sp", xc[:], xn2T[:, :, t0:t0 + TCH], writes=["xc"])
                for e in range(8):
                    P.mm_group([(ps8[:, 7, :], sel[0:8, e * 128:(e + 1) * 128],
                                 cT[0:8, t0:t0 + TCH])],
                               writes=["b7"], reads=["sel", "cT"])
                    P.op("act", "activation", cbc[:, e, :], ps8[:, 7, :], AF.Identity,
                         reads=["b7"], writes=["cbc"])
                for e in range(8):
                    es = e % 2
                    for dc in range(DC):
                        P.op("dve", "tensor_mul", xe[:, es, dc, :], xc[:, dc, :],
                             cbc[:, e, :], reads=["xc", "cbc"], writes=[f"xe{es}"])
                    for f in range(8):
                        P.mm_group([(ps8[:, f, :], ew[:, e, dc, f * 128:(f + 1) * 128],
                                     xe[:, es, dc, :]) for dc in range(DC)],
                                   writes=[f"b{f}"], reads=["ew", f"xe{es}"],
                                   flags=[((e == 0 and dc2 == 0), False)
                                          for dc2 in range(DC)])
                for f in range(8):
                    P.mm_group([(ps8[:, f, :], eb[0:8, f * 128:(f + 1) * 128],
                                 cTb[0:8, t0:t0 + TCH])],
                               writes=[f"b{f}"], reads=["eb", "cTb"],
                               flags=[(False, True)])
                for f in range(8):
                    P.op("act", "activation", g2c[:, f, :], ps8[:, f, :], AF.Silu,
                         reads=[f"b{f}"], writes=["g2c"])
                P.dma("sp", g2T[:, :, t0:t0 + TCH], g2c[:], reads=["g2c"])
            P.barrier_all()

    def phase_final(self):
        nc, P = self.nc, self.P
        x1T = self.ap("x1T").rearrange("(c p) t -> p c t", p=128)
        attoT = self.ap("attoT").rearrange("(c p) t -> p c t", p=128)
        g2T = self.ap("g2T").rearrange("(c p) t -> p c t", p=128)
        outT = self.ap("outT").rearrange("(c p) t -> p c t", p=128)
        with ExitStack() as ctx:
            x1c = self.sbuf(ctx, "p9_x1c", [128, DC, TCH], F32)
            atc = self.sbuf(ctx, "p9_atc", [128, DC, TCH], BF16)
            g2c = self.sbuf(ctx, "p9_g2c", [128, DC, TCH], BF16)
            t = self.sbuf(ctx, "p9_t", [128, DC, TCH], F32)
            oc = self.sbuf(ctx, "p9_oc", [128, DC, TCH], F32)
            for ch in range(NCH):
                t0 = ch * TCH
                P.dma("sp", x1c[:], x1T[:, :, t0:t0 + TCH], writes=["x1c"])
                P.dma("sp", atc[:], attoT[:, :, t0:t0 + TCH], writes=["atc"])
                P.dma("sp", g2c[:], g2T[:, :, t0:t0 + TCH], writes=["g2c"])
                P.op("dve", "tensor_mul", t[:], atc[:], g2c[:],
                     reads=["atc", "g2c"], writes=["t"])
                P.op("dve", "tensor_add", oc[:], x1c[:], t[:],
                     reads=["x1c", "t"], writes=["oc"])
                P.dma("sp", outT[:, :, t0:t0 + TCH], oc[:], reads=["oc"])
            P.barrier_all()

    def phase_attn(self):
        nc, P = self.nc, self.P
        xn2T = self.ap("xn2T").rearrange("(c p) t -> p c t", p=128)
        attoT = self.ap("attoT").rearrange("(c p) t -> p c t", p=128)
        QC = 4
        with ExitStack() as ctx:
            bq = self.sbuf(ctx, "p7_bq", [128, DC], F32)
            bk = self.sbuf(ctx, "p7_bk", [128, DC], F32)
            bo = self.sbuf(ctx, "p7_bo", [128, DC], F32)
            bvb = self.sbuf(ctx, "p7_bvb", [1, 1024], BF16)
            Erb = self.sbuf(ctx, "p7_Erb", [128, L], BF16)
            on_b = self.sbuf(ctx, "p7_onb", [128, 128], BF16)
            on_f = self.sbuf(ctx, "p7_onf", [128, 64], F32)
            P.dma("sp", bq[:], self.ap("bq_t"), writes=["bq"])
            P.dma("sp", bk[:], self.ap("bk_t"), writes=["bk"])
            P.dma("sp", bo[:], self.ap("bo_t"), writes=["bo"])
            P.dma("pl", bvb[:], self.ap("bv_row"), writes=["bvb"])
            P.dma("pl", Erb[0:64, :], self.ap("ErT"), writes=["Erb"])
            P.dma("pl", Erb[64:128, :], self.ap("ErT"), writes=["Erb"])
            P.op("dve", "memset", on_b[:], 1.0, writes=["onb"])
            P.op("dve", "memset", on_f[:], 1.0, writes=["onf"])
            for bi in range(2):
                tb = bi * L
                with ExitStack() as ctxB:
                    Wq = self.sbuf(ctxB, f"p7_Wq_{bi}", [128, DC, 1024], BF16)
                    Wk = self.sbuf(ctxB, f"p7_Wk_{bi}", [128, DC, 1024], BF16)
                    Wv = self.sbuf(ctxB, f"p7_Wv_{bi}", [128, DC, 1024], BF16)
                    Wo = self.sbuf(ctxB, f"p7_Wo_{bi}", [128, DC, 1024], BF16)
                    for nm, t in [("WqT", Wq), ("WkT", Wk), ("WvT", Wv), ("WoT", Wo)]:
                        P.dma("pl", t[:],
                              self.ap(nm).rearrange("(c p) f -> p c f", p=128),
                              writes=[nm])
                    xn2c = self.sbuf(ctxB, f"p7_x_{bi}", [128, DC, L], BF16)
                    Ksb = self.sbuf(ctxB, f"p7_K_{bi}", [128, DC, L], BF16)
                    Vau = self.sbuf(ctxB, f"p7_V_{bi}", [128, 16, NHEAD * 65], BF16)
                    Qs = self.sbuf(ctxB, f"p7_Q_{bi}", [128, 2, TCH], BF16)
                    tmp8 = self.sbuf(ctxB, f"p7_tmp8_{bi}", [64, 2, TCH], BF16)
                    Pt = self.sbuf(ctxB, f"p7_P_{bi}", [128, 16, TCH], BF16)
                    rL = self.sbuf(ctxB, f"p7_rL_{bi}", [128, TCH], F32)
                    rbc = self.sbuf(ctxB, f"p7_rbc_{bi}", [64, TCH], F32)
                    att = self.sbuf(ctxB, f"p7_att_{bi}", [128, DC, TCH], BF16)
                    aoc = Pt[:, 0:DC, :]
                    ps = self.psum(ctxB, f"p7_ps_{bi}", [128, 8, TCH])
                    P.dma("sp", xn2c[:], xn2T[:, :, tb:tb + L], writes=["xn2c"])
                    for ft in range(DC):
                        for cc in range(4):
                            sl = cc % 2
                            P.mm_group([(ps[:, sl, :],
                                         Wk[:, dc, ft * 128:(ft + 1) * 128],
                                         xn2c[:, dc, cc * TCH:(cc + 1) * TCH])
                                        for dc in range(DC)],
                                       writes=[f"b{sl}"], reads=["WkT", "xn2c"])
                            P.op("act", "activation",
                                 Ksb[:, ft, cc * TCH:(cc + 1) * TCH], ps[:, sl, :],
                                 AF.Identity, bias=bk[:, ft:ft + 1],
                                 reads=[f"b{sl}", "bk"], writes=["Ksb"])
                        P.op("dve", "tensor_add", Ksb[0:64, ft, :], Ksb[0:64, ft, :],
                             Erb[0:64, :], reads=["Ksb", "Erb"], writes=["Ksb"])
                        P.op("dve", "tensor_add", Ksb[64:128, ft, :], Ksb[64:128, ft, :],
                             Erb[64:128, :], reads=["Ksb", "Erb"], writes=["Ksb"])
                    Vh = Vau[:].rearrange("p t (h u) -> p t h u", u=65)
                    P.op("dve", "memset", Vh[:, :, :, 64:65], 1.0, writes=["Vau_one"])
                    for tt in range(16):
                        for i in range(2):
                            sl = 2 + (tt % 2)
                            pairs = [(ps[:, sl, :],
                                      xn2c[:, dc, tt * 128:(tt + 1) * 128],
                                      Wv[:, dc, i * 512:(i + 1) * 512])
                                     for dc in range(DC)]
                            pairs.append((ps[:, sl, :], on_b[0:1, 0:128],
                                          bvb[0:1, i * 512:(i + 1) * 512]))
                            P.mm_group(pairs, writes=[f"b{sl}"],
                                       reads=["xn2c", "WvT", "onb", "bvb"])
                            for hh in range(8):
                                h = i * 8 + hh
                                P.op("dve", "tensor_copy",
                                     Vau[:, tt, h * 65:h * 65 + 64],
                                     ps[:, sl, hh * 64:(hh + 1) * 64],
                                     reads=[f"b{sl}"], writes=["Vau"])
                    for qc in range(QC):
                        q0 = qc * TCH
                        for hp in range(8):
                            qp = hp % 2
                            qb = 4 if qp == 0 else 2
                            P.mm_group([(ps[:, qb, :],
                                         Wq[:, dc, hp * 128:(hp + 1) * 128],
                                         xn2c[:, dc, q0:q0 + TCH])
                                        for dc in range(DC)],
                                       writes=[f"b{qb}"], reads=["WqT", "xn2c"])
                            P.op("act", "activation", Qs[:, qp, :], ps[:, qb, :],
                                 AF.Identity, bias=bq[:, hp:hp + 1],
                                 reads=[f"b{qb}", "bq"], writes=[f"Qs{qp}"])
                            for ho in range(2):
                                h = 2 * hp + ho
                                hb = ho * 64
                                hr = h % 2
                                for kt in range(16):
                                    b0_ = 0 if (kt % 4) < 2 else 6
                                    sl = b0_ + (kt % 2)
                                    P.mm_group([(ps[:, sl, :],
                                                 Ksb[hb:hb + 64, hp,
                                                     kt * 128:(kt + 1) * 128],
                                                 Qs[hb:hb + 64, qp, :])],
                                               writes=[f"b{sl}"],
                                               reads=["Ksb", f"Qs{qp}"])
                                    if kt % 2 == 1:
                                        P.op("act", "activation",
                                             Pt[:, kt - 1:kt + 1, :],
                                             ps[:, b0_:b0_ + 2, :],
                                             AF.Exp, scale=0.125,
                                             reads=[f"b{b0_}", f"b{b0_ + 1}"],
                                             writes=["Pt"])
                                P.mm_group([(ps[0:65, 3, :],
                                             Vau[:, kt2, h * 65:h * 65 + 65],
                                             Pt[:, kt2, :]) for kt2 in range(16)],
                                           writes=["b3"], reads=["Vau", "Pt"])
                                P.op("dve", "reciprocal", rL[64:65, :],
                                     ps[64:65, 3, :], reads=["b3"], writes=["rL"])
                                P.mm_group([(ps[0:64, 5, :], on_f[64:65, 0:64],
                                             rL[64:65, :])],
                                           writes=["b5"], reads=["onf", "rL"])
                                P.op("act", "activation", rbc[0:64, :],
                                     ps[0:64, 5, :], AF.Identity,
                                     reads=["b5"], writes=["rbc"])
                                if ho == 0:
                                    P.op("dve", "tensor_mul", att[0:64, hp, :],
                                         ps[0:64, 3, :], rbc[0:64, :],
                                         reads=["b3", "rbc"], writes=["att"])
                                else:
                                    tp = hp % 2
                                    P.op("dve", "tensor_mul", tmp8[0:64, tp, :],
                                         ps[0:64, 3, :], rbc[0:64, :],
                                         reads=["b3", "rbc"], writes=[f"tmp8{tp}"])
                                    P.dma("sp", att[64:128, hp, :], tmp8[0:64, tp, :],
                                          reads=[f"tmp8{tp}"], writes=["att"])
                        for ft in range(DC):
                            sl = 6 + (ft % 2)
                            P.mm_group([(ps[:, sl, :],
                                         Wo[:, dc, ft * 128:(ft + 1) * 128],
                                         att[:, dc, :]) for dc in range(DC)],
                                       writes=[f"b{sl}"], reads=["WoT", "att"])
                            P.op("act", "activation", aoc[:, ft, :], ps[:, sl, :],
                                 AF.Identity, bias=bo[:, ft:ft + 1],
                                 reads=[f"b{sl}", "bo"], writes=["Pt"])
                        P.dma("sp", attoT[:, :, tb + q0:tb + q0 + TCH], aoc[:],
                              reads=["Pt"])
                    P.barrier_all()
            P.barrier_all()


import numpy as np
import ml_dtypes
from contextlib import ExitStack
from concourse.bass_utils import run_bass_kernel_spmd

BF = ml_dtypes.bfloat16
NP2BIR = {np.dtype(np.float32): F32, np.dtype(BF): BF16}
NCORES = 8

SCRATCH = {
    "xn1T": ((D, TOK), BF),
    "xn1fT": ((D, TOK), np.float32),
    "xn2fT": ((D, TOK), np.float32),
    "hT": ((D, TOK), BF),
    "sbrT": ((D, TOK), BF),
    "comb0T": ((8, TOK), np.float32),
    "comb1T": ((8, TOK), np.float32),
    "accT": ((D, TOK), np.float32),
    "x1T": ((D, TOK), np.float32),
    "xn2T": ((D, TOK), BF),
    "attoT": ((D, TOK), BF),
    "g2T": ((D, TOK), BF),
    "in_w_bf": ((128, 8, 6144), BF),
    "w1b": ((8, 128, 8, 2816), BF),
    "w3b": ((8, 128, 8, 2816), BF),
    "w2b": ((8, 128, 22, 1024), BF),
    "outT": ((D, TOK), np.float32),
}

PHASES = [
    ("phase_ln1", ["x0T", "ln0_gamma_t", "ln0_beta_t", "in_w", "w1T", "w3T",
                   "w2T"], [],
     ["xn1T", "xn1fT", "hT", "in_w_bf", "w1b", "w3b", "w2b"]),
    ("phase_mamba", ["in_b_t", "conv_w_t", "conv_b_t", "A_log_t",
                     "B_p_t", "C_p_t", "D_p_t", "out_w", "out_b_t", "identnk"],
     ["xn1T", "hT", "in_w_bf"], ["sbrT"]),
    ("phase_gate0", ["gate_w_T", "ident", "ones1_f32", "tiebk"],
     ["xn1fT"], ["comb0T"]),
    ("phase_moe0", ["sel"], ["xn1T", "comb0T", "w1b", "w3b", "w2b"], ["accT"]),
    ("phase_combine0", ["x0T"], ["sbrT", "accT"], ["x1T", "xn2T", "xn2fT"]),
    ("phase_gate1", ["gate_W", "gate_b_row", "ones1_f32", "ident", "tiebk"],
     ["xn2fT"], ["comb1T"]),
    ("phase_attn", ["WqT", "WkT", "WvT", "WoT", "bq_t", "bk_t", "bo_t",
                    "bv_row", "ErT"], ["xn2T"], ["attoT"]),
    ("phase_moe1", ["expert_W", "expert_b", "sel"], ["xn2T", "comb1T"], ["g2T"]),
    ("phase_final", [], ["x1T", "attoT", "g2T"], ["outT"]),
]


def tcol(vec, nparts=128):
    n = vec.shape[0]
    return np.ascontiguousarray(vec.reshape(n // nparts, nparts).T)


def host_inputs_core(inp, core):
    b0 = core * 2
    x0 = np.asarray(inp["L0_attn_x"])[b0:b0 + 2]
    x0T = np.ascontiguousarray(x0.reshape(TOK, D).T)
    sel = np.zeros((8, 1024), dtype=np.float32)
    for e in range(8):
        sel[e, e * 128:(e + 1) * 128] = 1.0
    m = {
        "x0T": x0T,
        "ln0_gamma_t": tcol(np.asarray(inp["L0_attn_norm_gamma"])),
        "ln0_beta_t": tcol(np.asarray(inp["L0_attn_norm_beta"])),
        "in_w": np.ascontiguousarray(np.asarray(inp["L0_attn_in_w"])),
        "in_b_t": tcol(np.asarray(inp["L0_attn_in_b"])),
        "conv_w_t": np.ascontiguousarray(
            np.asarray(inp["L0_attn_conv_w"]).reshape(2048, 3)
            .reshape(16, 128, 3).transpose(1, 0, 2)),
        "conv_b_t": tcol(np.asarray(inp["L0_attn_conv_b"])),
        "A_log_t": np.ascontiguousarray(
            np.asarray(inp["L0_attn_A_log"]).reshape(16, 128, 64)
            .transpose(1, 0, 2).reshape(128, 1024)),
        "B_p_t": np.ascontiguousarray(
            np.asarray(inp["L0_attn_B_p"]).reshape(16, 128, 64)
            .transpose(1, 0, 2).reshape(128, 1024)),
        "C_p_t": np.ascontiguousarray(
            np.asarray(inp["L0_attn_C_p"]).reshape(16, 128, 64)
            .transpose(1, 0, 2).reshape(128, 1024)),
        "D_p_t": tcol(np.asarray(inp["L0_attn_D_p"])),
        "out_w": np.ascontiguousarray(np.asarray(inp["L0_attn_out_w"])),
        "out_b_t": tcol(np.asarray(inp["L0_attn_out_b"])),
        "gate_w_T": np.ascontiguousarray(np.asarray(inp["L0_ffn_gate_w"]).T),
        "w1T": np.ascontiguousarray(np.asarray(inp["L0_ffn_w1"]).transpose(0, 2, 1)),
        "w3T": np.ascontiguousarray(np.asarray(inp["L0_ffn_w3"]).transpose(0, 2, 1)),
        "w2T": np.ascontiguousarray(np.asarray(inp["L0_ffn_w2"]).transpose(0, 2, 1)),
        "WqT": np.ascontiguousarray(np.asarray(inp["L1_attn_W_q"]).T),
        "WkT": np.ascontiguousarray(np.asarray(inp["L1_attn_W_k"]).T),
        "WvT": np.ascontiguousarray(np.asarray(inp["L1_attn_W_v"]).T),
        "WoT": np.ascontiguousarray(np.asarray(inp["L1_attn_W_o"]).T),
        "bq_t": tcol(np.asarray(inp["L1_attn_b_q"])),
        "bk_t": tcol(np.asarray(inp["L1_attn_b_k"])),
        "bo_t": tcol(np.asarray(inp["L1_attn_b_o"])),
        "bv_row": np.ascontiguousarray(np.asarray(inp["L1_attn_b_v"])[None, :]),
        "ErT": np.ascontiguousarray(
            np.asarray(inp["L1_attn_Er"])[2952:5000][::-1].T),
        "gate_W": np.ascontiguousarray(np.asarray(inp["L1_ffn_gate_W"])),
        "gate_b_row": np.ascontiguousarray(np.asarray(inp["L1_ffn_gate_b"])[None, :]),
        "expert_W": np.ascontiguousarray(np.asarray(inp["L1_ffn_expert_W"])),
        "expert_b": np.ascontiguousarray(np.asarray(inp["L1_ffn_expert_b"])),
        "ident": np.eye(128, dtype=np.float32),
        "identnk": np.broadcast_to(np.eye(128, dtype=np.float32).astype(
            ml_dtypes.bfloat16)[:, None, :], (128, 24, 128)),
        "ones1_f32": np.ones((1, 128), dtype=np.float32),
        "sel": sel,
        "tiebk": np.arange(7, -1, -1, dtype=np.float32)[None, :] * 1e-8,
    }
    return {k: np.ascontiguousarray(v) for k, v in m.items()}


def _build_phase(name, innames, sin, souts):
    import concourse.bass as bass
    nc = bass.Bass("TRN2", target_bir_lowering=False)
    io = {}
    for nm in innames:
        shp, dt = INPUT_SPECS[nm]
        io[nm] = nc.dram_tensor(nm, list(shp), NP2BIR[np.dtype(dt)],
                                kind="ExternalInput")
    for nm in sin:
        shp, dt = SCRATCH[nm]
        io[nm] = nc.dram_tensor(nm, list(shp), NP2BIR[np.dtype(dt)],
                                kind="ExternalInput")
    for nm in souts:
        shp, dt = SCRATCH[nm]
        io[nm] = nc.dram_tensor(nm, list(shp), NP2BIR[np.dtype(dt)],
                                kind="ExternalOutput")
    with ExitStack() as ctx:
        P = Prog(nc, ctx)
        getattr(MK(nc, P, io), name)()
    return nc


INPUT_SPECS = None
MODE = "fused"


def _build_fused():
    import concourse.bass as bass
    nc = bass.Bass("TRN2", target_bir_lowering=False)
    io = {}
    for nm, (shp, dt) in INPUT_SPECS.items():
        io[nm] = nc.dram_tensor(nm, list(shp), NP2BIR[np.dtype(dt)],
                                kind="ExternalInput")
    for nm, (shp, dt) in SCRATCH.items():
        kind = "ExternalOutput" if nm == "outT" else "Internal"
        io[nm] = nc.dram_tensor(nm, list(shp), NP2BIR[np.dtype(dt)], kind=kind)
    with ExitStack() as ctx:
        P = Prog(nc, ctx)
        mk = MK(nc, P, io)
        for name, _, _, _ in PHASES:
            getattr(mk, name)()
    return nc


def kernel(**inputs):
    global INPUT_SPECS
    maps = [host_inputs_core(inputs, c) for c in range(NCORES)]
    INPUT_SPECS = {k: (v.shape, v.dtype) for k, v in maps[0].items()}
    if MODE == "fused":
        nc = _build_fused()
        res = run_bass_kernel_spmd(nc, maps, core_ids=list(range(NCORES)))
        outs = [res.results[c]["outT"].T.reshape(2, L, D) for c in range(NCORES)]
        return np.ascontiguousarray(np.concatenate(outs, axis=0))
    state = [dict() for _ in range(NCORES)]
    for name, innames, sin, souts in PHASES:
        nc = _build_phase(name, innames, sin, souts)
        in_maps = []
        for c in range(NCORES):
            im = {k: maps[c][k] for k in innames}
            for k in sin:
                im[k] = state[c][k]
            in_maps.append(im)
        res = run_bass_kernel_spmd(nc, in_maps, core_ids=list(range(NCORES)))
        for c in range(NCORES):
            for k in souts:
                state[c][k] = res.results[c][k]
    outs = []
    for c in range(NCORES):
        outs.append(state[c]["outT"].T.reshape(2, L, D))
    return np.ascontiguousarray(np.concatenate(outs, axis=0))
